# version 04
# baseline (speedup 1.0000x reference)
import numpy as np
from contextlib import ExitStack

import concourse.bass as bass
import concourse.mybir as mybir
from concourse.bass_utils import run_bass_kernel_spmd

F32 = mybir.dt.float32
BF16 = mybir.dt.bfloat16
FP8 = mybir.dt.float8e4
AF = mybir.ActivationFunctionType
ALU = mybir.AluOpType
AX = mybir.AxisListType

NCORES = 8
TT = 8
D = 2048
DC = 16
S = 4096
NH = 4
QCW = 512
NQC = S // QCW
C_SHIFT = 20.0
SCALE = 0.08838834764831845
INV_D = 1.0 / 2048.0
EPS = 1e-5
GROUPS = [[0, 1, 2, 3], [4, 5, 6, 7]]


class Builder:
    def __init__(self, nc, paranoid=False):
        self.nc = nc
        self.paranoid = paranoid
        self.q = {k: [] for k in ("sync", "gpsimd", "vector", "scalar", "tensor")}
        self.sem = {}
        self.cnt = {}
        self.bank_state = [None] * 8
        self.risky = {}

    def add_sem(self, ctx, name):
        self.sem[name] = ctx.enter_context(self.nc.semaphore(name))
        self.cnt[name] = 0

    def flush(self, blockname):
        with self.nc.Block(blockname) as block:
            for eng, fns in self.q.items():
                if fns:
                    getattr(block, eng)(lambda e, fns=tuple(fns): [f(e) for f in fns])
        self.q = {k: [] for k in self.q}

    def op(self, eng, semname, fn, waits=(), inc=1, chain=False, risky=False):
        w = [(s, v) for (s, v) in waits if v > 0]
        if chain or (self.paranoid and eng == "vector"):
            prev = self.cnt[semname]
            if prev > 0:
                w.append((semname, prev))
        pend = self.risky.pop(eng, None)
        if pend is not None:
            w.append(pend)
        self.cnt[semname] += inc
        if risky:
            self.risky[eng] = (semname, self.cnt[semname])
        sems = self.sem

        def run(e, w=tuple(w), fn=fn, semname=semname, inc=inc):
            seen = {}
            for (s, v) in w:
                if seen.get(s, -1) < v:
                    e.wait_ge(sems[s], v)
                    seen[s] = v
            fn(e).then_inc(sems[semname], inc)
        self.q[eng].append(run)
        return self.cnt[semname]

    def wait(self, eng, waits):
        w = [(s, v) for (s, v) in waits if v > 0]
        sems = self.sem

        def run(e):
            for (s, v) in w:
                e.wait_ge(sems[s], v)
        self.q[eng].append(run)

    def dma(self, eng, semname, out, in_, waits=()):
        return self.op(eng, semname, lambda e: e.dma_start(out=out, in_=in_),
                       waits=waits, inc=16)

    def use_bank(self, k):
        st = self.bank_state[k]
        return [st] if st is not None else []

    def done_bank(self, k, semname, val):
        self.bank_state[k] = (semname, val)


def build(paranoid=False, taps=False):
    nc = bass.Bass("TRN2", target_bir_lowering=False)

    x_sh = nc.dram_tensor("x_sh", [1024, D], F32, kind="ExternalInput")
    wq_t = nc.dram_tensor("wq_t", [D, 512], F32, kind="ExternalInput")
    wk_t = nc.dram_tensor("wk_t", [D, 512], F32, kind="ExternalInput")
    wv_t = nc.dram_tensor("wv_t", [D, 512], F32, kind="ExternalInput")
    wo_t = nc.dram_tensor("wo_t", [512, D], F32, kind="ExternalInput")
    gwt = nc.dram_tensor("gwt", [128, 128], F32, kind="ExternalInput")
    gb = nc.dram_tensor("gb", [128, 1], F32, kind="ExternalInput")
    wf = nc.dram_tensor("wf", [D, D], F32, kind="ExternalInput")
    out_sh = nc.dram_tensor("out_sh", [1024, D], F32, kind="ExternalOutput")

    zag_in = nc.dram_tensor("zag_in", [D, 1024], BF16)
    zag_out = nc.dram_tensor("zag_out", [4 * D, 1024], BF16)
    rs_in = [nc.dram_tensor(f"rs_in{p}", [S, QCW], BF16) for p in range(4)]
    rs_out = [nc.dram_tensor(f"rs_out{p}", [1024, QCW], BF16) for p in range(4)]
    y_spill = nc.dram_tensor("y_spill", [1024, D], F32)
    wdeq_spill = nc.dram_tensor("wdeq_spill", [DC, 128, D], BF16)
    if taps:
        dbg_am = nc.dram_tensor("dbg_am", [DC, 128, DC], F32, kind="ExternalOutput")
        dbg_r2 = nc.dram_tensor("dbg_r2", [DC, 128, DC], F32, kind="ExternalOutput")
        dbg_q8 = nc.dram_tensor("dbg_q8", [DC, 128, D], FP8, kind="ExternalOutput")
        dbg_dq = nc.dram_tensor("dbg_dq", [DC, 128, D], BF16, kind="ExternalOutput")
        dbg_y = nc.dram_tensor("dbg_y", [1024, D], F32, kind="ExternalOutput")
        dbg_wd = nc.dram_tensor("dbg_wd", [DC, 128, D], BF16, kind="ExternalOutput")
        dbg_zo = nc.dram_tensor("dbg_zo", [4 * D, 1024], BF16, kind="ExternalOutput")
        dbg_q = nc.dram_tensor("dbg_q", [128, NH * S], BF16, kind="ExternalOutput")
        dbg_k = nc.dram_tensor("dbg_k", [128, NH * S], BF16, kind="ExternalOutput")
        dbg_v = nc.dram_tensor("dbg_v", [128, 32 * 512], BF16, kind="ExternalOutput")
        dbg_m = nc.dram_tensor("dbg_m", [128, NH * S], BF16, kind="ExternalOutput")
        dbg_g = nc.dram_tensor("dbg_g", [128, 2 * S], BF16, kind="ExternalOutput")
        dbg_rs = nc.dram_tensor("dbg_rs", [1024, QCW], BF16, kind="ExternalOutput")

    b = Builder(nc, paranoid=paranoid)
    with ExitStack() as ctx:
        for s in ("ld", "gs", "cc", "pe", "dv", "sc", "gp", "lx",
                  "lwf0", "lwf1", "lz0", "lz1", "lw30", "lw31", "lcs",
                  "lwo", "lwd", "ly0", "ly1", "lr0", "lr1", "gr0", "gr1",
                  "sy", "sw"):
            b.add_sem(ctx, s)

        ps = [ctx.enter_context(nc.psum_tensor(f"ps{k}", [128, QCW], F32))
              for k in range(6)]
        pt = [ctx.enter_context(nc.psum_tensor(f"pt{k}", [128, 1024], BF16))
              for k in range(2)]
        banks = ps + pt

        ident_f = ctx.enter_context(nc.sbuf_tensor("ident_f", [128, 128], F32))
        ident_b = ctx.enter_context(nc.sbuf_tensor("ident_b", [128, 128], BF16))
        mask_add = ctx.enter_context(nc.sbuf_tensor("mask_add", [128, 128], F32))
        ones_b = ctx.enter_context(nc.sbuf_tensor("ones_b", [128, 1], BF16))
        ones1f = ctx.enter_context(nc.sbuf_tensor("ones1f", [1, 128], F32))
        negC = ctx.enter_context(nc.sbuf_tensor("negC", [128, 1], F32))
        epsb = ctx.enter_context(nc.sbuf_tensor("epsb", [128, 1], F32))
        gb_sb = ctx.enter_context(nc.sbuf_tensor("gb_sb", [128, 1], F32))
        gwt_f = ctx.enter_context(nc.sbuf_tensor("gwt_f", [128, 128], F32))
        gwt_b = ctx.enter_context(nc.sbuf_tensor("gwt_b", [128, 128], BF16))
        rsum = ctx.enter_context(nc.sbuf_tensor("rsum", [128, 2], F32))
        nmean = ctx.enter_context(nc.sbuf_tensor("nmean", [128, 2], F32))
        ssq = ctx.enter_context(nc.sbuf_tensor("ssq", [128, 2], F32))
        sdvv = ctx.enter_context(nc.sbuf_tensor("sdvv", [128, 2], F32))
        rstd = ctx.enter_context(nc.sbuf_tensor("rstd", [128, 2], F32))
        amax = ctx.enter_context(nc.sbuf_tensor("amax", [128, 2, DC], F32))
        r2 = ctx.enter_context(nc.sbuf_tensor("r2", [128, 2, DC], F32))
        s2 = ctx.enter_context(nc.sbuf_tensor("s2", [128, 2, DC], F32))
        sq_scrap = ctx.enter_context(nc.sbuf_tensor("sq_scrap", [128, 2, D], BF16))

        for (tname, val) in ((ident_f, 0.0), (ident_b, 0.0), (mask_add, 0.0),
                             (ones_b, 1.0), (ones1f, 1.0), (negC, -C_SHIFT),
                             (epsb, EPS)):
            b.op("gpsimd", "gp", lambda e, t=tname, v=val: e.memset(t[:], v), chain=True)
        b.op("gpsimd", "gp", lambda e: e.affine_select(
            out=ident_f[:], in_=ident_f[:], compare_op=ALU.not_equal,
            fill=1.0, base=0, pattern=[[-1, 128]], channel_multiplier=1), chain=True)
        b.op("gpsimd", "gp", lambda e: e.affine_select(
            out=ident_b[:], in_=ident_b[:], compare_op=ALU.not_equal,
            fill=1.0, base=0, pattern=[[-1, 128]], channel_multiplier=1), chain=True)
        gm = b.op("gpsimd", "gp", lambda e: e.affine_select(
            out=mask_add[:], in_=mask_add[:], compare_op=ALU.is_ge,
            fill=-1.0e30, base=0, pattern=[[1, 128]], channel_multiplier=-1), chain=True)
        b.dma("sync", "ld", gb_sb[:], gb[:, :])
        lgw = b.dma("sync", "ld", gwt_f[:], gwt[:, :])
        GWT_B = b.op("vector", "dv", lambda e: e.tensor_copy(out=gwt_b[:], in_=gwt_f[:]),
                     waits=[("ld", lgw)])

        ctxA = ExitStack()
        x_str = ctxA.enter_context(nc.sbuf_tensor("x_str", [128, TT, D], F32))
        wdeqT = ctxA.enter_context(nc.sbuf_tensor("wdeqT", [128, DC, D], BF16))
        xh = ctxA.enter_context(nc.sbuf_tensor("xh", [128, 2, D], F32))
        xq8 = ctxA.enter_context(nc.sbuf_tensor("xq8", [128, 2, D], FP8))
        xdq = ctxA.enter_context(nc.sbuf_tensor("xdq", [128, 2, D], BF16))
        xdT = ctxA.enter_context(nc.sbuf_tensor("xdT", [128, 2, DC, 128], BF16))
        scrA = dict(xh=xh, xq8=xq8, xdq=xdq, xdT=xdT, sq=sq_scrap)
        ctxA0 = ExitStack()
        wfn = ctxA0.enter_context(nc.sbuf_tensor("wfn", [128, 2, D], F32))
        wq8 = ctxA0.enter_context(nc.sbuf_tensor("wq8", [128, 2, D], FP8))
        wdq = ctxA0.enter_context(nc.sbuf_tensor("wdq", [128, 2, D], BF16))

        ld_x = [b.dma("sync", "lx", x_str[:, t, :], x_sh[128 * t:128 * (t + 1), :])
                for t in range(TT)]

        sc_wq8 = [0] * DC
        dv_wdq = [0] * DC
        pe_wtr = [[0] * 4 for _ in range(DC)]
        GP_INIT = gm
        for i in range(DC):
            p = i % 2
            ldwf = b.dma("sync", f"lwf{p}", wfn[:, p, :], wf[128 * i:128 * (i + 1), :],
                         waits=([("dv", sc_wq8[i - 2])] if i >= 2 else []))
            b.op("vector", "dv", lambda e, p=p: e.tensor_reduce(
                out=amax[:, p, :],
                in_=wfn[:, p, :].rearrange("p (c k) -> p c k", k=128),
                axis=AX.X, op=ALU.max, apply_absolute_value=True),
                waits=[(f"lwf{p}", ldwf)], risky=True)
            b.op("vector", "dv", lambda e, p=p: e.reciprocal(
                out=r2[:, p, :], in_=amax[:, p, :]), risky=True)
            b.op("vector", "dv", lambda e, p=p: e.tensor_scalar_mul(
                out=r2[:, p, :], in0=r2[:, p, :], scalar1=224.0))
            dv_r2w = b.op("vector", "dv", lambda e, p=p: e.tensor_scalar_mul(
                out=s2[:, p, :], in0=amax[:, p, :], scalar1=1.0 / 224.0))
            for blk in range(DC):
                sc_wq8[i] = b.op("vector", "dv", lambda e, p=p, blk=blk: e.tensor_scalar(
                    out=wq8[:, p, 128 * blk:128 * (blk + 1)],
                    in0=wfn[:, p, 128 * blk:128 * (blk + 1)],
                    scalar1=r2[:, p, blk:blk + 1], scalar2=None, op0=ALU.mult))
            for blk in range(DC):
                w_ = [("pe", pe_wtr[i - 2][3])] if (i >= 2 and blk == 0) else []
                dv_wdq[i] = b.op("vector", "dv", lambda e, p=p, blk=blk: e.tensor_scalar(
                    out=wdq[:, p, 128 * blk:128 * (blk + 1)],
                    in0=wq8[:, p, 128 * blk:128 * (blk + 1)],
                    scalar1=s2[:, p, blk:blk + 1], scalar2=None, op0=ALU.mult),
                    waits=w_)
            if taps:
                b.dma("gpsimd", "gs", dbg_am[i, :, :], amax[:, p, :],
                      waits=[("dv", dv_wdq[i])])
                b.dma("gpsimd", "gs", dbg_r2[i, :, :], r2[:, p, :])
                b.dma("gpsimd", "gs", dbg_q8[i, :, :], wq8[:, p, :])
                b.dma("gpsimd", "gs", dbg_dq[i, :, :], wdq[:, p, :])
            for k in range(4):
                bank = 6 + (k % 2)
                for j in range(4):
                    c = 4 * k + j
                    w_ = [("dv", dv_wdq[i])] if (k == 0 and j == 0) else []
                    if j == 0:
                        w_ += b.use_bank(bank)
                    if (i == 0 and k == 0 and j == 0) or b.paranoid:
                        w_.append(("gp", GP_INIT))
                    pe_wtr[i][k] = b.op("tensor", "pe", lambda e, p=p, c=c, bank=bank, j=j: e.transpose(
                        banks[bank][:, 128 * j:128 * (j + 1)],
                        wdq[:, p, 128 * c:128 * (c + 1)], ident_b[:]),
                        waits=w_)
                dvev = b.op("vector", "dv", lambda e, i=i, k=k, bank=bank: e.tensor_copy(
                    out=wdeqT[:, 4 * k:4 * (k + 1), 128 * i:128 * (i + 1)],
                    in_=banks[bank][:, 0:512].rearrange("p (j k) -> p j k", k=128)),
                    waits=[("pe", pe_wtr[i][k])])
                b.done_bank(bank, "dv", dvev)
        W_DONE = dvev

        def emit_ffn(xs, wT, w_ready, x_ready, scr, store=None):
            xh = scr["xh"]; xq8 = scr["xq8"]; xdq = scr["xdq"]
            xdT = scr["xdT"]; sq_scrap = scr["sq"]
            sc_ssqv = [0] * TT
            sc_q8 = [0] * TT
            dv_xdq = [0] * TT
            dv_ev = [[0] * 4 for _ in range(TT)]
            dv_res = [[0] * 4 for _ in range(TT)]
            pe_tr = [[0] * 4 for _ in range(TT)]
            pe_mm = [[0] * 4 for _ in range(TT)]
            st_out = []
            for t in range(TT):
                p = t % 2
                w_ = list(x_ready[t])
                if t >= 2:
                    w_.append(("sc", sc_ssqv[t - 2]))
                b.op("vector", "dv", lambda e, t=t, p=p: e.tensor_reduce(
                    out=rsum[:, p:p + 1], in_=xs[:, t, :], axis=AX.X,
                    op=ALU.add, negate=True), waits=w_, risky=True)
                dv_nm = b.op("vector", "dv", lambda e, p=p: e.tensor_scalar_mul(
                    out=nmean[:, p:p + 1], in0=rsum[:, p:p + 1], scalar1=INV_D))
                sc_ssqv[t] = b.op("scalar", "sc", lambda e, t=t, p=p: e.activation(
                    out=sq_scrap[:, p, :], in_=xs[:, t, :], func=AF.Square,
                    bias=nmean[:, p:p + 1], scale=1.0, accum_out=ssq[:, p:p + 1]),
                    waits=[("dv", dv_nm)] + ([("gp", gm)] if t == 0 else []),
                    chain=True)
                sc_sd = b.op("scalar", "sc", lambda e, p=p: e.activation(
                    out=sdvv[:, p:p + 1], in_=ssq[:, p:p + 1], func=AF.Sqrt,
                    bias=epsb[:, 0:1], scale=INV_D), chain=True)
                b.op("vector", "dv", lambda e, p=p: e.reciprocal(
                    out=rstd[:, p:p + 1], in_=sdvv[:, p:p + 1]), waits=[("sc", sc_sd)],
                    risky=True)
                b.op("vector", "dv", lambda e, t=t, p=p: e.tensor_scalar(
                    out=xh[:, p, :], in0=xs[:, t, :], scalar1=nmean[:, p:p + 1],
                    scalar2=rstd[:, p:p + 1], op0=ALU.add, op1=ALU.mult), waits=w_)
                b.op("vector", "dv", lambda e, p=p: e.tensor_reduce(
                    out=amax[:, p, :],
                    in_=xh[:, p, :].rearrange("p (c k) -> p c k", k=128),
                    axis=AX.X, op=ALU.max, apply_absolute_value=True), risky=True)
                b.op("vector", "dv", lambda e, p=p: e.reciprocal(
                    out=r2[:, p, :], in_=amax[:, p, :]), risky=True)
                b.op("vector", "dv", lambda e, p=p: e.tensor_scalar_mul(
                    out=r2[:, p, :], in0=r2[:, p, :], scalar1=224.0))
                dv_r2 = b.op("vector", "dv", lambda e, p=p: e.tensor_scalar_mul(
                    out=s2[:, p, :], in0=amax[:, p, :], scalar1=1.0 / 224.0))
                for blk in range(DC):
                    sc_q8[t] = b.op("vector", "dv", lambda e, p=p, blk=blk: e.tensor_scalar(
                        out=xq8[:, p, 128 * blk:128 * (blk + 1)],
                        in0=xh[:, p, 128 * blk:128 * (blk + 1)],
                        scalar1=r2[:, p, blk:blk + 1], scalar2=None, op0=ALU.mult))
                for blk in range(DC):
                    w_ = [("pe", pe_tr[t - 2][3])] if (t >= 2 and blk == 0) else []
                    dv_xdq[t] = b.op("vector", "dv", lambda e, p=p, blk=blk: e.tensor_scalar(
                        out=xdq[:, p, 128 * blk:128 * (blk + 1)],
                        in0=xq8[:, p, 128 * blk:128 * (blk + 1)],
                        scalar1=s2[:, p, blk:blk + 1], scalar2=None, op0=ALU.mult),
                        waits=w_)
                for k in range(4):
                    bank = 6 + (k % 2)
                    for j in range(4):
                        c = 4 * k + j
                        w_ = [("dv", dv_xdq[t])] if (k == 0 and j == 0) else []
                        if j == 0:
                            w_ += b.use_bank(bank)
                        if b.paranoid:
                            w_.append(("gp", gm))
                        pe_tr[t][k] = b.op("tensor", "pe", lambda e, p=p, c=c, bank=bank, j=j: e.transpose(
                            banks[bank][:, 128 * j:128 * (j + 1)],
                            xdq[:, p, 128 * c:128 * (c + 1)], ident_b[:]), waits=w_)
                    w_ev = [("pe", pe_tr[t][k])]
                    if t >= 2 and k == 0:
                        w_ev.append(("pe", pe_mm[t - 2][3]))
                    dv_ev[t][k] = b.op("vector", "dv", lambda e, k=k, p=p, bank=bank: e.tensor_copy(
                        out=xdT[:, p, 4 * k:4 * (k + 1), :],
                        in_=banks[bank][:, 0:512].rearrange("p (j k) -> p j k", k=128)),
                        waits=w_ev)
                    b.done_bank(bank, "dv", dv_ev[t][k])
                def emit_resid(tt):
                    for oc in range(4):
                        dv_res[tt][oc] = b.op("vector", "dv",
                                              lambda e, tt=tt, oc=oc: e.scalar_tensor_tensor(
                                                  out=xs[:, tt, 512 * oc:512 * (oc + 1)],
                                                  in0=ps[oc][:, 0:512], scalar=0.5,
                                                  in1=xs[:, tt, 512 * oc:512 * (oc + 1)],
                                                  op0=ALU.mult, op1=ALU.add),
                                              waits=[("pe", pe_mm[tt][oc])])
                        b.done_bank(oc, "dv", dv_res[tt][oc])
                    if store is not None:
                        st_out.append(b.dma("gpsimd", "gs",
                                            store[128 * tt:128 * (tt + 1), :],
                                            xs[:, tt, :], waits=[("dv", dv_res[tt][3])]))
                if t >= 1:
                    emit_resid(t - 1)
                for c in range(DC):
                    for oc in range(4):
                        w_ = []
                        if c == 0:
                            w_ = [("dv", dv_ev[t][3]), w_ready] + b.use_bank(oc)
                        pe_mm[t][oc] = b.op("tensor", "pe", lambda e, p=p, c=c, oc=oc: e.matmul(
                            ps[oc][:, 0:512], xdT[:, p, c, :],
                            wT[:, c, 512 * oc:512 * (oc + 1)],
                            start=(c == 0), stop=(c == DC - 1)), waits=w_)
            emit_resid(TT - 1)
            return dv_res, st_out

        dv_res1, _ = emit_ffn(x_str, wdeqT, ("dv", W_DONE),
                              [[("lx", 16 * TT)] for t in range(TT)], scrA)
        b.flush("ph01_wq_ffn1")
        ctxA0.close()
        zT_sb = ctxA.enter_context(nc.sbuf_tensor("zT_sb", [128, DC, 1024], BF16))

        pe_ztr = [[0] * 4 for _ in range(TT)]
        dv_zev = [[0] * 4 for _ in range(TT)]
        sc_ss2 = [0] * TT
        for t in range(TT):
            p = t % 2
            w_ = [("sc", sc_ss2[t - 2])] if t >= 2 else []
            b.op("vector", "dv", lambda e, t=t, p=p: e.tensor_reduce(
                out=rsum[:, p:p + 1], in_=x_str[:, t, :], axis=AX.X,
                op=ALU.add, negate=True), waits=w_, risky=True)
            dv_nm = b.op("vector", "dv", lambda e, p=p: e.tensor_scalar_mul(
                out=nmean[:, p:p + 1], in0=rsum[:, p:p + 1], scalar1=INV_D))
            sc_ss2[t] = b.op("scalar", "sc", lambda e, t=t, p=p: e.activation(
                out=sq_scrap[:, p, :], in_=x_str[:, t, :], func=AF.Square,
                bias=nmean[:, p:p + 1], scale=1.0, accum_out=ssq[:, p:p + 1]),
                waits=[("dv", dv_nm)], chain=True)
            sc_sd = b.op("scalar", "sc", lambda e, p=p: e.activation(
                out=sdvv[:, p:p + 1], in_=ssq[:, p:p + 1], func=AF.Sqrt,
                bias=epsb[:, 0:1], scale=INV_D), chain=True)
            b.op("vector", "dv", lambda e, p=p: e.reciprocal(
                out=rstd[:, p:p + 1], in_=sdvv[:, p:p + 1]), waits=[("sc", sc_sd)],
                risky=True)
            w_ = [("pe", pe_ztr[t - 2][3])] if t >= 2 else []
            dv_z = b.op("vector", "dv", lambda e, t=t, p=p: e.tensor_scalar(
                out=xh[:, p, :], in0=x_str[:, t, :], scalar1=nmean[:, p:p + 1],
                scalar2=rstd[:, p:p + 1], op0=ALU.add, op1=ALU.mult), waits=w_)
            for k in range(4):
                bank = 4 + (k % 2)
                for j in range(4):
                    c = 4 * k + j
                    w_ = [("dv", dv_z)] if (k == 0 and j == 0) else []
                    if t == 0 and k == 0 and j == 0:
                        w_.append(("gp", gm))
                    if j == 0:
                        w_ += b.use_bank(bank)
                    pe_ztr[t][k] = b.op("tensor", "pe", lambda e, p=p, c=c, bank=bank, j=j: e.transpose(
                        ps[bank][:, 128 * j:128 * (j + 1)],
                        xh[:, p, 128 * c:128 * (c + 1)], ident_f[:]), waits=w_)
                dv_zev[t][k] = b.op("vector", "dv", lambda e, t=t, k=k, bank=bank: e.tensor_copy(
                    out=zT_sb[:, 4 * k:4 * (k + 1), 128 * t:128 * (t + 1)],
                    in_=ps[bank][:, 0:512].rearrange("p (j k) -> p j k", k=128)),
                    waits=[("pe", pe_ztr[t][k])])
                b.done_bank(bank, "dv", dv_zev[t][k])
            b.dma("scalar", "sy", y_spill[128 * t:128 * (t + 1), :], x_str[:, t, :],
                  waits=[("dv", dv_res1[t][3])])
        SW_SPILL = 0
        for c in range(DC):
            SW_SPILL = b.dma("sync", "sw", wdeq_spill[c, :, :], wdeqT[:, c, :],
                             waits=[("dv", W_DONE)])
        gs_z = 0
        for c in range(DC):
            gs_z = b.dma("gpsimd", "gs", zag_in[128 * c:128 * (c + 1), :], zT_sb[:, c, :],
                         waits=[("dv", dv_zev[TT - 1][3])])
        GS_SPILL = b.cnt["gs"]
        cc_ag = b.op("gpsimd", "cc", lambda e: e.collective_compute(
            "AllGather", ALU.bypass, replica_groups=GROUPS,
            ins=[zag_in.ap().opt()], outs=[zag_out.ap().opt()]),
            waits=[("gs", gs_z)], chain=True)
        FENCE2 = [(s, b.cnt[s]) for s in ("sw", "sy", "gs", "pe", "dv", "sc")]
        b.flush("ph2_z_ag")

        ctxA.close()
        ctxBm = ExitStack()
        m_t = ctxBm.enter_context(nc.sbuf_tensor("m_t", [128, NH, S], BF16, side="right"))
        ctxB = ExitStack()
        q_t = ctxB.enter_context(nc.sbuf_tensor("q_t", [128, NH, S], BF16))
        k_t = ctxB.enter_context(nc.sbuf_tensor("k_t", [128, NH, S], BF16))
        v_t = ctxB.enter_context(nc.sbuf_tensor("v_t", [128, 32, 512], BF16))
        ctxB3 = ExitStack()
        zt_q = ctxB3.enter_context(nc.sbuf_tensor("zt_q", [128, DC, 512], BF16))
        wqf = ctxB3.enter_context(nc.sbuf_tensor("wqf", [128, 2, 512], F32))
        wq_b = ctxB3.enter_context(nc.sbuf_tensor("wq_b", [128, DC, 512], BF16))
        wk_b = ctxB3.enter_context(nc.sbuf_tensor("wk_b", [128, DC, 512], BF16))
        wv_b = ctxB3.enter_context(nc.sbuf_tensor("wv_b", [128, DC, 512], BF16))

        dv_w3 = 0
        nw = 0
        for (wsrc, wbbuf) in ((wq_t, wq_b), (wk_t, wk_b), (wv_t, wv_b)):
            for c in range(DC):
                p = nw % 2
                ldw = b.dma("sync", f"lw3{p}", wqf[:, p, :],
                            wsrc[128 * c:128 * (c + 1), :],
                            waits=([("dv", dv_w3)] if nw >= 2 else FENCE2))
                dv_w3 = b.op("vector", "dv",
                             lambda e, p=p, c=c, wbbuf=wbbuf: e.tensor_copy(
                                 out=wbbuf[:, c, :], in_=wqf[:, p, :]),
                             waits=[(f"lw3{p}", ldw)])
                nw += 1
        pe_zread = [0] * DC
        for hq in range(8):
            qq, half = divmod(hq, 2)
            ldz = 0
            for c in range(DC):
                w_ = [("pe", pe_zread[c])] if hq >= 1 else list(FENCE2)
                ldz = b.dma("sync", f"lz{hq % 2}", zt_q[:, c, :],
                            zag_out[D * qq + 128 * c:D * qq + 128 * (c + 1),
                                    512 * half:512 * (half + 1)],
                            waits=w_ + [("cc", cc_ag)])
            zq_ready = (f"lz{hq % 2}", ldz)
            for (wb, dst) in ((wq_b, q_t), (wk_b, k_t)):
                pe_h = [0] * NH
                for c in range(DC):
                    for h in range(NH):
                        w_ = []
                        if c == 0:
                            w_ = [zq_ready, ("dv", dv_w3)] + b.use_bank(h)
                        pe_h[h] = b.op("tensor", "pe",
                                       lambda e, c=c, h=h, wb=wb: e.matmul(
                                           ps[h][:, 0:512],
                                           wb[:, c, 128 * h:128 * (h + 1)],
                                           zt_q[:, c, :],
                                           start=(c == 0), stop=(c == DC - 1)),
                                       waits=w_)
                for h in range(NH):
                    dvx = b.op("vector", "dv",
                               lambda e, h=h, hq=hq, dst=dst: e.tensor_copy(
                                   out=dst[:, h, 512 * hq:512 * (hq + 1)],
                                   in_=ps[h][:, 0:512]), waits=[("pe", pe_h[h])])
                    b.done_bank(h, "dv", dvx)
            pe_g = [0] * 4
            for c in range(DC):
                for u in range(4):
                    w_ = []
                    if c == 0:
                        w_ = [zq_ready, ("dv", dv_w3)] + b.use_bank(u)
                    pe_g[u] = b.op("tensor", "pe",
                                   lambda e, c=c, u=u: e.matmul(
                                       ps[u][:, 0:512],
                                       zt_q[:, c, 128 * u:128 * (u + 1)],
                                       wv_b[:, c, 0:512],
                                       start=(c == 0), stop=(c == DC - 1)), waits=w_)
                    pe_zread[c] = pe_g[u]
            for u in range(4):
                dvx = b.op("vector", "dv", lambda e, hq=hq, u=u: e.tensor_copy(
                    out=v_t[:, 4 * hq + u, :], in_=ps[u][:, 0:512]),
                    waits=[("pe", pe_g[u])])
                b.done_bank(u, "dv", dvx)
        PH3_DV = b.cnt["dv"]
        b.flush("ph3_qkv")
        ctxB3.close()

        ctxB4 = ExitStack()
        gate_h = ctxB4.enter_context(nc.sbuf_tensor("gate_h", [128, 2, S], BF16))
        av_sb = ctxB4.enter_context(nc.sbuf_tensor("av_sb", [128, 2, S], BF16))
        p_t = ctxB4.enter_context(nc.sbuf_tensor("p_t", [128, 2, QCW], BF16))
        cs_row = ctxB4.enter_context(nc.sbuf_tensor("cs_row", [1, S], F32))
        cs128 = ctxB4.enter_context(nc.sbuf_tensor("cs128", [128, 32], F32))
        rc128 = ctxB4.enter_context(nc.sbuf_tensor("rc128", [128, 32], F32))
        recip_b = ctxB4.enter_context(nc.sbuf_tensor("recip_b", [128, S], F32))

        dv_mt = [0] * NH
        pt_free = [0, 0]
        for h in range(NH):
            ph = h % 2
            scg = 0
            for qc in range(NQC):
                bank = 4 + (qc % 2)
                w_ = b.use_bank(bank)
                if qc == 0:
                    w_ += [("gp", gm), ("dv", GWT_B)]
                    if h == 0:
                        w_.append(("dv", PH3_DV))
                    if h >= 2:
                        w_.append(("dv", dv_mt[h - 2]))
                peg = b.op("tensor", "pe", lambda e, h=h, qc=qc, bank=bank: e.matmul(
                    ps[bank][:, 0:512], gwt_b[:], q_t[:, h, 512 * qc:512 * (qc + 1)],
                    start=True, stop=True), waits=w_)
                scg = b.op("scalar", "sc", lambda e, ph=ph, qc=qc, bank=bank: e.activation(
                    out=gate_h[:, ph, 512 * qc:512 * (qc + 1)], in_=ps[bank][:, 0:512],
                    func=AF.Sigmoid, bias=gb_sb[:, 0:1], scale=1.0),
                    waits=[("pe", peg)] + ([("ld", 32), ("gp", gm)]
                                           if (h == 0 and qc == 0) else []),
                    chain=True)
                b.done_bank(bank, "sc", scg)
            dva = 0
            dvc = 0
            for qc in range(NQC):
                klim = 4 * qc + 4
                for ki in range(klim):
                    off = max(0, 128 * ki - 512 * qc)
                    bank = ki % 2
                    bufp = ki % 2
                    pes = b.op("tensor", "pe",
                               lambda e, h=h, qc=qc, ki=ki, off=off, bank=bank: e.matmul(
                                   ps[bank][:, off:512],
                                   k_t[:, h, 128 * ki:128 * (ki + 1)],
                                   q_t[:, h, 512 * qc + off:512 * (qc + 1)],
                                   start=True, stop=True), waits=b.use_bank(bank))
                    if ki >= 4 * qc:
                        wm = [("pe", pes)]
                        if h == 0 and qc == 0 and ki == 0:
                            wm.append(("gp", gm))
                        pes_m = b.op("vector", "dv", lambda e, bank=bank, off=off: e.tensor_tensor(
                            out=ps[bank][:, off:off + 128],
                            in0=ps[bank][:, off:off + 128],
                            in1=mask_add[:], op=ALU.add), waits=wm)
                    else:
                        pes_m = 0
                    if off:
                        dv_ms = b.op("vector", "dv", lambda e, bufp=bufp, off=off: e.memset(
                            p_t[:, bufp, 0:off], 0.0), waits=[("pe", pt_free[bufp])])
                    else:
                        dv_ms = 0
                    w_e = [("pe", pes), ("pe", pt_free[bufp]), ("dv", pes_m)]
                    sc_e = b.op("scalar", "sc",
                                lambda e, bufp=bufp, off=off, bank=bank: e.activation(
                                    out=p_t[:, bufp, off:512], in_=ps[bank][:, off:512],
                                    func=AF.Exp, bias=negC[:, 0:1], scale=SCALE),
                                chain=True, waits=w_e)
                    b.done_bank(bank, "sc", sc_e)
                    w_av = [("sc", sc_e), ("dv", dv_ms)]
                    if ki == 0:
                        w_av += b.use_bank(2)
                    pe_av = b.op("tensor", "pe", lambda e, h=h, ki=ki, bufp=bufp, klim=klim: e.matmul(
                        ps[2][:, 0:512], v_t[:, ki, 128 * h:128 * (h + 1)],
                        p_t[:, bufp, 0:512], start=(ki == 0), stop=(ki == klim - 1)),
                        waits=w_av)
                    w_cs = [] if ki else b.use_bank(3)
                    pe_csl = b.op("tensor", "pe", lambda e, ki=ki, bufp=bufp, klim=klim: e.matmul(
                        ps[3][0:1, 0:512], ones_b[:], p_t[:, bufp, 0:512],
                        start=(ki == 0), stop=(ki == klim - 1)), waits=w_cs)
                    pt_free[bufp] = pe_csl
                dva = b.op("vector", "dv", lambda e, ph=ph, qc=qc: e.tensor_copy(
                    out=av_sb[:, ph, 512 * qc:512 * (qc + 1)], in_=ps[2][:, 0:512]),
                    waits=[("pe", pe_av)])
                b.done_bank(2, "dv", dva)
                dvc = b.op("vector", "dv", lambda e, qc=qc: e.tensor_copy(
                    out=cs_row[0:1, 512 * qc:512 * (qc + 1)], in_=ps[3][0:1, 0:512]),
                    waits=[("pe", pe_csl)])
                b.done_bank(3, "dv", dvc)
            ld1 = b.dma("sync", "lcs", cs128[:, :], cs_row[0:1, :], waits=[("dv", dvc)])
            dvr = b.op("vector", "dv", lambda e: e.reciprocal(out=rc128[:, :], in_=cs128[:, :]),
                       waits=[("lcs", ld1)], risky=True)
            ld2 = b.dma("sync", "lcs", cs_row[0:1, :], rc128[:, :], waits=[("dv", dvr)])
            dv_rb = 0
            for qc in range(NQC):
                bank = 4 + (qc % 2)
                w_ = b.use_bank(bank) + [("lcs", ld2)]
                peb = b.op("tensor", "pe", lambda e, qc=qc, bank=bank: e.matmul(
                    ps[bank][:, 0:512], ones1f[:],
                    cs_row[0:1, 512 * qc:512 * (qc + 1)],
                    start=True, stop=True), waits=w_)
                dv_rb = b.op("vector", "dv", lambda e, qc=qc, bank=bank: e.tensor_copy(
                    out=recip_b[:, 512 * qc:512 * (qc + 1)], in_=ps[bank][:, 0:512]),
                    waits=[("pe", peb)])
                b.done_bank(bank, "dv", dv_rb)
            dvt = b.op("vector", "dv", lambda e, ph=ph: e.tensor_tensor(
                out=recip_b[:, :], in0=av_sb[:, ph, :], in1=recip_b[:, :], op=ALU.mult),
                waits=[("dv", dv_rb), ("sc", scg)])
            dv_mt[h] = b.op("vector", "dv", lambda e, h=h, ph=ph: e.tensor_tensor(
                out=m_t[:, h, :], in0=recip_b[:, :], in1=gate_h[:, ph, :], op=ALU.mult),
                waits=[("dv", dvt)])
        if taps:
            gt = b.dma("gpsimd", "gs", dbg_q[:, :],
                       q_t[:, :, :].rearrange("p a b -> p (a b)"),
                       waits=[("dv", PH3_DV)])
            gt = b.dma("gpsimd", "gs", dbg_k[:, :],
                       k_t[:, :, :].rearrange("p a b -> p (a b)"))
            gt = b.dma("gpsimd", "gs", dbg_v[:, :],
                       v_t[:, :, :].rearrange("p a b -> p (a b)"))
            gt = b.dma("gpsimd", "gs", dbg_m[:, :],
                       m_t[:, :, :].rearrange("p a b -> p (a b)"),
                       waits=[("dv", dv_mt[NH - 1])])
            gt = b.dma("gpsimd", "gs", dbg_g[:, :],
                       gate_h[:, :, :].rearrange("p a b -> p (a b)"))
            b.wait("gpsimd", [("gs", gt)])
        b.flush("ph4_attn")
        ctxB4.close()
        ctxB.close()

        ctxC = ExitStack()
        x_str2 = ctxC.enter_context(nc.sbuf_tensor("x_str2", [128, TT, D], F32))
        wdeqT2 = ctxC.enter_context(nc.sbuf_tensor("wdeqT2", [128, DC, D], BF16))
        ctxC0 = ExitStack()
        wof = ctxC0.enter_context(nc.sbuf_tensor("wof", [128, 1, D], F32))
        wo_b = ctxC0.enter_context(nc.sbuf_tensor("wo_b", [128, 4, D], BF16))
        rs_buf = ctxC0.enter_context(nc.sbuf_tensor("rs_buf", [128, 2, QCW], BF16))

        dv_wo = 0
        for c in range(4):
            ldw = b.dma("sync", "lwo", wof[:, 0, :], wo_t[128 * c:128 * (c + 1), :],
                        waits=([("dv", dv_wo)] if c >= 1 else [("dv", dv_mt[NH - 1])]))
            dv_wo = b.op("vector", "dv", lambda e, c=c: e.tensor_copy(
                out=wo_b[:, c, :], in_=wof[:, 0, :]), waits=[("lwo", ldw)])
        cc_rs = [0] * 4
        gr_last = [0, 0]
        for e2p in range(4):
            for qt in range(32):
                bank = qt % 2
                peo = 0
                for c in range(4):
                    w_ = []
                    if c == 0:
                        w_ = [("dv", dv_wo), ("dv", dv_mt[NH - 1]),
                              ("dv", dv_mt[NH - 2])] + b.use_bank(bank)
                    peo = b.op("tensor", "pe",
                               lambda e, e2p=e2p, qt=qt, c=c, bank=bank: e.matmul(
                                   ps[bank][:, 0:512], m_t[:, c, 128 * qt:128 * (qt + 1)],
                                   wo_b[:, c, 512 * e2p:512 * (e2p + 1)],
                                   start=(c == 0), stop=(c == 3)), waits=w_)
                dvo = b.op("vector", "dv", lambda e, bank=bank: e.tensor_copy(
                    out=rs_buf[:, bank, :], in_=ps[bank][:, 0:512]),
                    waits=[("pe", peo),
                           (f"gr{bank}", gr_last[bank])])
                b.done_bank(bank, "dv", dvo)
                gr_last[bank] = b.dma("gpsimd", f"gr{bank}",
                                      rs_in[e2p][128 * qt:128 * (qt + 1), :],
                                      rs_buf[:, bank, :], waits=[("dv", dvo)])
            cc_rs[e2p] = b.op("gpsimd", "cc", lambda e, e2p=e2p: e.collective_compute(
                "ReduceScatter", ALU.add, replica_groups=GROUPS,
                ins=[rs_in[e2p].ap().opt()], outs=[rs_out[e2p].ap().opt()]),
                waits=[("gr0", gr_last[0]), ("gr1", gr_last[1])], chain=True)
        if taps:
            gt = b.dma("gpsimd", "gs", dbg_y[:, :], y_spill[:, :],
                       waits=[("gs", GS_SPILL)])
            gt = b.dma("gpsimd", "gs", dbg_wd[:, :, :], wdeq_spill[:, :, :])
            gt = b.dma("gpsimd", "gs", dbg_zo[:, :], zag_out[:, :],
                       waits=[("cc", cc_ag)])
            gt = b.dma("gpsimd", "gs", dbg_rs[:, :], rs_out[0][:, :],
                       waits=[("cc", cc_rs[3])])
            b.wait("gpsimd", [("gs", gt)])
        b.flush("ph5_oproj_rs")
        ctxC0.close()
        ctxBm.close()
        y_buf = ctxC.enter_context(nc.sbuf_tensor("y_buf", [128, 2, D], F32))
        rs_sb = ctxC.enter_context(nc.sbuf_tensor("rs_sb", [128, 2, 4, QCW], BF16))
        xh2 = ctxC.enter_context(nc.sbuf_tensor("xh2", [128, 2, D], F32))
        xq82 = ctxC.enter_context(nc.sbuf_tensor("xq82", [128, 2, D], FP8))
        xdq2 = ctxC.enter_context(nc.sbuf_tensor("xdq2", [128, 2, D], BF16))
        xdT2 = ctxC.enter_context(nc.sbuf_tensor("xdT2", [128, 2, DC, 128], BF16))
        scrC = dict(xh=xh2, xq8=xq82, xdq=xdq2, xdT=xdT2, sq=sq_scrap)

        ld_wd = 0
        for c in range(DC):
            ld_wd = b.dma("sync", "lwd", wdeqT2[:, c, :], wdeq_spill[c, :, :],
                          waits=[("sw", SW_SPILL), ("dv", dv_mt[NH - 1])])
        dv_x2 = [0] * TT
        for t in range(TT):
            p = t % 2
            w_ = [("dv", dv_x2[t - 2])] if t >= 2 else []
            ldy = b.dma("sync", f"ly{p}", y_buf[:, p, :],
                        y_spill[128 * t:128 * (t + 1), :],
                        waits=w_ + [("sy", 16 * TT), ("dv", dv_mt[NH - 1])])
            ldr = 0
            for pp in range(4):
                ldr = b.dma("sync", f"lr{p}", rs_sb[:, p, pp, :],
                            rs_out[pp][128 * t:128 * (t + 1), :],
                            waits=w_ + [("cc", cc_rs[pp])])
            for pp in range(4):
                dv_x2[t] = b.op("vector", "dv", lambda e, t=t, p=p, pp=pp: e.tensor_tensor(
                    out=x_str2[:, t, 512 * pp:512 * (pp + 1)],
                    in0=y_buf[:, p, 512 * pp:512 * (pp + 1)],
                    in1=rs_sb[:, p, pp, :], op=ALU.add),
                    waits=[(f"ly{p}", ldy), (f"lr{p}", ldr)])
        b.flush("ph6_resid")

        _, st_out = emit_ffn(x_str2, wdeqT2, ("lwd", ld_wd),
                             [[("dv", dv_x2[t])] for t in range(TT)], scrC,
                             store=out_sh)
        b.wait("gpsimd", [("gs", st_out[-1])])
        b.flush("ph7_ffn2")
        ctxC.close()

    return nc


_cache = {}


def kernel(L0_attn_x, L0_attn_q_w, L0_attn_k_w, L0_attn_v_w, L0_attn_o_w,
           L0_attn_gate_w, L0_attn_gate_b, L0_ffn_weight):
    x = np.ascontiguousarray(np.asarray(L0_attn_x, dtype=np.float32).reshape(8192, 2048))
    qw = np.asarray(L0_attn_q_w, dtype=np.float32)
    kw = np.asarray(L0_attn_k_w, dtype=np.float32)
    vw = np.asarray(L0_attn_v_w, dtype=np.float32)
    ow = np.asarray(L0_attn_o_w, dtype=np.float32)
    gw = np.asarray(L0_attn_gate_w, dtype=np.float32)
    gbv = np.ascontiguousarray(np.asarray(L0_attn_gate_b, dtype=np.float32).reshape(128, 1))
    fw = np.ascontiguousarray(np.asarray(L0_ffn_weight, dtype=np.float32))
    owt = np.ascontiguousarray(ow.T)
    gwt_host = np.ascontiguousarray(gw.T)

    if "nc" not in _cache:
        _cache["nc"] = build()
    nc = _cache["nc"]

    in_maps = []
    for c in range(NCORES):
        bb, r = divmod(c, 4)
        sl = slice(512 * r, 512 * (r + 1))
        in_maps.append({
            "x_sh": np.ascontiguousarray(x[4096 * bb + 1024 * r: 4096 * bb + 1024 * (r + 1)]),
            "wq_t": np.ascontiguousarray(qw[sl, :].T),
            "wk_t": np.ascontiguousarray(kw[sl, :].T),
            "wv_t": np.ascontiguousarray(vw[sl, :].T),
            "wo_t": np.ascontiguousarray(owt[sl, :]),
            "gwt": gwt_host,
            "gb": gbv,
            "wf": fw,
        })
    res = run_bass_kernel_spmd(nc, in_maps, core_ids=list(range(NCORES)))
    out = np.concatenate([np.asarray(res.results[c]["out_sh"]) for c in range(NCORES)],
                         axis=0)
    return out.reshape(2, 4096, 2048).astype(np.float32)


# version 05
# speedup vs baseline: 1.4078x; 1.4078x over previous
import numpy as np
from contextlib import ExitStack

import concourse.bass as bass
import concourse.mybir as mybir
from concourse.bass_utils import run_bass_kernel_spmd

F32 = mybir.dt.float32
BF16 = mybir.dt.bfloat16
FP8 = mybir.dt.float8e4
AF = mybir.ActivationFunctionType
ALU = mybir.AluOpType
AX = mybir.AxisListType

NCORES = 8
TT = 8
D = 2048
DC = 16
S = 4096
NH = 4
QCW = 512
NQC = S // QCW
C_SHIFT = 20.0
SCALE = 0.08838834764831845
INV_D = 1.0 / 2048.0
EPS = 1e-5
GROUPS = [[0, 1, 2, 3], [4, 5, 6, 7]]


class Builder:
    def __init__(self, nc, paranoid=False):
        self.nc = nc
        self.paranoid = paranoid
        self.q = {k: [] for k in ("sync", "gpsimd", "vector", "scalar", "tensor")}
        self.sem = {}
        self.cnt = {}
        self.bank_state = [None] * 8
        self.risky = {}

    def add_sem(self, ctx, name):
        self.sem[name] = ctx.enter_context(self.nc.semaphore(name))
        self.cnt[name] = 0

    def flush(self, blockname):
        with self.nc.Block(blockname) as block:
            for eng, fns in self.q.items():
                if fns:
                    getattr(block, eng)(lambda e, fns=tuple(fns): [f(e) for f in fns])
        self.q = {k: [] for k in self.q}

    def op(self, eng, semname, fn, waits=(), inc=1, chain=False, risky=False):
        w = [(s, v) for (s, v) in waits if v > 0]
        if chain or (self.paranoid and eng == "vector"):
            prev = self.cnt[semname]
            if prev > 0:
                w.append((semname, prev))
        pend = self.risky.pop(eng, None)
        if pend is not None:
            w.append(pend)
        self.cnt[semname] += inc
        if risky:
            self.risky[eng] = (semname, self.cnt[semname])
        sems = self.sem

        def run(e, w=tuple(w), fn=fn, semname=semname, inc=inc):
            seen = {}
            for (s, v) in w:
                if seen.get(s, -1) < v:
                    e.wait_ge(sems[s], v)
                    seen[s] = v
            fn(e).then_inc(sems[semname], inc)
        self.q[eng].append(run)
        return self.cnt[semname]

    def wait(self, eng, waits):
        w = [(s, v) for (s, v) in waits if v > 0]
        sems = self.sem

        def run(e):
            for (s, v) in w:
                e.wait_ge(sems[s], v)
        self.q[eng].append(run)

    def dma(self, eng, semname, out, in_, waits=()):
        return self.op(eng, semname, lambda e: e.dma_start(out=out, in_=in_),
                       waits=waits, inc=16)

    def use_bank(self, k):
        st = self.bank_state[k]
        return [st] if st is not None else []

    def done_bank(self, k, semname, val):
        self.bank_state[k] = (semname, val)


def build(paranoid=False, taps=False):
    nc = bass.Bass("TRN2", target_bir_lowering=False)

    x_sh = nc.dram_tensor("x_sh", [1024, D], F32, kind="ExternalInput")
    wq_t = nc.dram_tensor("wq_t", [D, 512], F32, kind="ExternalInput")
    wk_t = nc.dram_tensor("wk_t", [D, 512], F32, kind="ExternalInput")
    wv_t = nc.dram_tensor("wv_t", [D, 512], F32, kind="ExternalInput")
    wo_t = nc.dram_tensor("wo_t", [512, D], F32, kind="ExternalInput")
    gwt = nc.dram_tensor("gwt", [128, 128], F32, kind="ExternalInput")
    gb = nc.dram_tensor("gb", [128, 1], F32, kind="ExternalInput")
    wf = nc.dram_tensor("wf", [D, D], F32, kind="ExternalInput")
    out_sh = nc.dram_tensor("out_sh", [1024, D], F32, kind="ExternalOutput")

    zag_in = [nc.dram_tensor(f"zag_in{i}", [D, QCW], BF16) for i in range(2)]
    zag_out = [nc.dram_tensor(f"zag_out{i}", [4 * D, QCW], BF16) for i in range(2)]
    rs_in = [nc.dram_tensor(f"rs_in{p}", [S, QCW], BF16) for p in range(4)]
    rs_out = [nc.dram_tensor(f"rs_out{p}", [1024, QCW], BF16) for p in range(4)]
    y_spill = nc.dram_tensor("y_spill", [1024, D], F32)
    wdeq_spill = nc.dram_tensor("wdeq_spill", [DC, 128, D], BF16)
    if taps:
        dbg_am = nc.dram_tensor("dbg_am", [DC, 128, DC], F32, kind="ExternalOutput")
        dbg_r2 = nc.dram_tensor("dbg_r2", [DC, 128, DC], F32, kind="ExternalOutput")
        dbg_q8 = nc.dram_tensor("dbg_q8", [DC, 128, D], FP8, kind="ExternalOutput")
        dbg_dq = nc.dram_tensor("dbg_dq", [DC, 128, D], BF16, kind="ExternalOutput")
        dbg_y = nc.dram_tensor("dbg_y", [1024, D], F32, kind="ExternalOutput")
        dbg_wd = nc.dram_tensor("dbg_wd", [DC, 128, D], BF16, kind="ExternalOutput")
        dbg_zo = nc.dram_tensor("dbg_zo", [4 * D, 1024], BF16, kind="ExternalOutput")
        dbg_q = nc.dram_tensor("dbg_q", [128, NH * S], BF16, kind="ExternalOutput")
        dbg_k = nc.dram_tensor("dbg_k", [128, NH * S], BF16, kind="ExternalOutput")
        dbg_v = nc.dram_tensor("dbg_v", [128, 32 * 512], BF16, kind="ExternalOutput")
        dbg_m = nc.dram_tensor("dbg_m", [128, NH * S], BF16, kind="ExternalOutput")
        dbg_g = nc.dram_tensor("dbg_g", [128, 2 * S], BF16, kind="ExternalOutput")
        dbg_rs = nc.dram_tensor("dbg_rs", [1024, QCW], BF16, kind="ExternalOutput")

    b = Builder(nc, paranoid=paranoid)
    with ExitStack() as ctx:
        for s in ("ld", "gs", "cc", "pe", "dv", "sc", "gp", "lx",
                  "lwf0", "lwf1", "lz0", "lz1", "lw30", "lw31", "lcs",
                  "lwo", "lwd", "ly0", "ly1", "lr0", "lr1", "gr0", "gr1",
                  "sy", "sw"):
            b.add_sem(ctx, s)

        P = [ctx.enter_context(nc.psum_tensor(f"P{k}", [128, 1024], F32))
             for k in range(3)]
        pt = [ctx.enter_context(nc.psum_tensor(f"pt{k}", [128, 1024], BF16))
              for k in range(2)]

        class PV:
            def __init__(self, t, off):
                self.t = t
                self.off = off

            def __getitem__(self, idx):
                rows, cols = idx
                start = self.off + (cols.start or 0)
                stop = self.off + (cols.stop if cols.stop is not None else QCW)
                return self.t[rows, start:stop]

        ps = [PV(P[k // 2], QCW * (k % 2)) for k in range(6)]
        banks = ps + pt

        ident_f = ctx.enter_context(nc.sbuf_tensor("ident_f", [128, 128], F32))
        ident_b = ctx.enter_context(nc.sbuf_tensor("ident_b", [128, 128], BF16))
        mask_add = ctx.enter_context(nc.sbuf_tensor("mask_add", [128, 128], F32))
        ones_b = ctx.enter_context(nc.sbuf_tensor("ones_b", [128, 1], BF16))
        ones1f = ctx.enter_context(nc.sbuf_tensor("ones1f", [1, 128], F32))
        negC = ctx.enter_context(nc.sbuf_tensor("negC", [128, 1], F32))
        epsb = ctx.enter_context(nc.sbuf_tensor("epsb", [128, 1], F32))
        gb_sb = ctx.enter_context(nc.sbuf_tensor("gb_sb", [128, 1], F32))
        gwt_f = ctx.enter_context(nc.sbuf_tensor("gwt_f", [128, 128], F32))
        gwt_b = ctx.enter_context(nc.sbuf_tensor("gwt_b", [128, 128], BF16))
        rsum = ctx.enter_context(nc.sbuf_tensor("rsum", [128, 2], F32))
        nmean = ctx.enter_context(nc.sbuf_tensor("nmean", [128, 2], F32))
        ssq = ctx.enter_context(nc.sbuf_tensor("ssq", [128, 2], F32))
        sdvv = ctx.enter_context(nc.sbuf_tensor("sdvv", [128, 2], F32))
        rstd = ctx.enter_context(nc.sbuf_tensor("rstd", [128, 2], F32))
        amax = ctx.enter_context(nc.sbuf_tensor("amax", [128, 2, DC], F32))
        r2 = ctx.enter_context(nc.sbuf_tensor("r2", [128, 2, DC], F32))
        s2 = ctx.enter_context(nc.sbuf_tensor("s2", [128, 2, DC], F32))
        sq_scrap = ctx.enter_context(nc.sbuf_tensor("sq_scrap", [128, 2, D], BF16))

        for (tname, val) in ((ident_f, 0.0), (ident_b, 0.0), (mask_add, 0.0),
                             (ones_b, 1.0), (ones1f, 1.0), (negC, -C_SHIFT),
                             (epsb, EPS)):
            b.op("gpsimd", "gp", lambda e, t=tname, v=val: e.memset(t[:], v), chain=True)
        b.op("gpsimd", "gp", lambda e: e.affine_select(
            out=ident_f[:], in_=ident_f[:], compare_op=ALU.not_equal,
            fill=1.0, base=0, pattern=[[-1, 128]], channel_multiplier=1), chain=True)
        b.op("gpsimd", "gp", lambda e: e.affine_select(
            out=ident_b[:], in_=ident_b[:], compare_op=ALU.not_equal,
            fill=1.0, base=0, pattern=[[-1, 128]], channel_multiplier=1), chain=True)
        gm = b.op("gpsimd", "gp", lambda e: e.affine_select(
            out=mask_add[:], in_=mask_add[:], compare_op=ALU.is_ge,
            fill=-1.0e30, base=0, pattern=[[1, 128]], channel_multiplier=-1), chain=True)
        b.dma("sync", "ld", gb_sb[:], gb[:, :])
        lgw = b.dma("sync", "ld", gwt_f[:], gwt[:, :])
        GWT_B = b.op("vector", "dv", lambda e: e.tensor_copy(out=gwt_b[:], in_=gwt_f[:]),
                     waits=[("ld", lgw)])

        ctxA = ExitStack()
        x_str = ctxA.enter_context(nc.sbuf_tensor("x_str", [128, TT, D], F32))
        wdeqT = ctxA.enter_context(nc.sbuf_tensor("wdeqT", [128, DC, D], BF16))
        xh = ctxA.enter_context(nc.sbuf_tensor("xh", [128, 2, D], F32))
        xq8 = ctxA.enter_context(nc.sbuf_tensor("xq8", [128, 2, D], FP8))
        xdq = ctxA.enter_context(nc.sbuf_tensor("xdq", [128, 2, D], BF16))
        xdT = ctxA.enter_context(nc.sbuf_tensor("xdT", [128, 2, DC, 128], BF16))
        scrA = dict(xh=xh, xq8=xq8, xdq=xdq, xdT=xdT, sq=sq_scrap)
        ctxA0 = ExitStack()
        wfn = ctxA0.enter_context(nc.sbuf_tensor("wfn", [128, 2, D], F32))
        wq8 = ctxA0.enter_context(nc.sbuf_tensor("wq8", [128, 2, D], FP8))
        wdq = ctxA0.enter_context(nc.sbuf_tensor("wdq", [128, 2, D], BF16))

        ld_x = [b.dma("sync", "lx", x_str[:, t, :], x_sh[128 * t:128 * (t + 1), :])
                for t in range(TT)]

        sc_wq8 = [0] * DC
        dv_wdq = [0] * DC
        pe_wtr = [[0] * 4 for _ in range(DC)]
        GP_INIT = gm
        for i in range(DC):
            p = i % 2
            ldwf = b.dma("sync", f"lwf{p}", wfn[:, p, :], wf[128 * i:128 * (i + 1), :],
                         waits=([("dv", sc_wq8[i - 2])] if i >= 2 else []))
            b.op("vector", "dv", lambda e, p=p: e.tensor_reduce(
                out=amax[:, p, :],
                in_=wfn[:, p, :].rearrange("p (c k) -> p c k", k=128),
                axis=AX.X, op=ALU.max, apply_absolute_value=True),
                waits=[(f"lwf{p}", ldwf)], risky=True)
            b.op("vector", "dv", lambda e, p=p: e.reciprocal(
                out=r2[:, p, :], in_=amax[:, p, :]), risky=True)
            b.op("vector", "dv", lambda e, p=p: e.tensor_scalar_mul(
                out=r2[:, p, :], in0=r2[:, p, :], scalar1=224.0))
            dv_r2w = b.op("vector", "dv", lambda e, p=p: e.tensor_scalar_mul(
                out=s2[:, p, :], in0=amax[:, p, :], scalar1=1.0 / 224.0))
            for blk in range(DC):
                sc_wq8[i] = b.op("vector", "dv", lambda e, p=p, blk=blk: e.tensor_scalar(
                    out=wq8[:, p, 128 * blk:128 * (blk + 1)],
                    in0=wfn[:, p, 128 * blk:128 * (blk + 1)],
                    scalar1=r2[:, p, blk:blk + 1], scalar2=None, op0=ALU.mult))
            for blk in range(DC):
                w_ = [("pe", pe_wtr[i - 2][3])] if (i >= 2 and blk == 0) else []
                dv_wdq[i] = b.op("vector", "dv", lambda e, p=p, blk=blk: e.tensor_scalar(
                    out=wdq[:, p, 128 * blk:128 * (blk + 1)],
                    in0=wq8[:, p, 128 * blk:128 * (blk + 1)],
                    scalar1=s2[:, p, blk:blk + 1], scalar2=None, op0=ALU.mult),
                    waits=w_)
            if taps:
                b.dma("gpsimd", "gs", dbg_am[i, :, :], amax[:, p, :],
                      waits=[("dv", dv_wdq[i])])
                b.dma("gpsimd", "gs", dbg_r2[i, :, :], r2[:, p, :])
                b.dma("gpsimd", "gs", dbg_q8[i, :, :], wq8[:, p, :])
                b.dma("gpsimd", "gs", dbg_dq[i, :, :], wdq[:, p, :])
            for k in range(4):
                bank = 6 + (k % 2)
                for j in range(4):
                    c = 4 * k + j
                    w_ = [("dv", dv_wdq[i])] if (k == 0 and j == 0) else []
                    if j == 0:
                        w_ += b.use_bank(bank)
                    if (i == 0 and k == 0 and j == 0) or b.paranoid:
                        w_.append(("gp", GP_INIT))
                    pe_wtr[i][k] = b.op("tensor", "pe", lambda e, p=p, c=c, bank=bank, j=j: e.transpose(
                        banks[bank][:, 128 * j:128 * (j + 1)],
                        wdq[:, p, 128 * c:128 * (c + 1)], ident_b[:]),
                        waits=w_)
                dvev = b.op("vector", "dv", lambda e, i=i, k=k, bank=bank: e.tensor_copy(
                    out=wdeqT[:, 4 * k:4 * (k + 1), 128 * i:128 * (i + 1)],
                    in_=banks[bank][:, 0:512].rearrange("p (j k) -> p j k", k=128)),
                    waits=[("pe", pe_wtr[i][k])])
                b.done_bank(bank, "dv", dvev)
        W_DONE = dvev

        def emit_ffn(xs, wT, w_ready, x_ready, scr, store=None):
            xh = scr["xh"]; xq8 = scr["xq8"]; xdq = scr["xdq"]
            xdT = scr["xdT"]; sq_scrap = scr["sq"]
            sc_ssqv = [0] * TT
            sc_q8 = [0] * TT
            dv_xdq = [0] * TT
            dv_ev = [[0] * 4 for _ in range(TT)]
            dv_res = [[0] * 4 for _ in range(TT)]
            pe_tr = [[0] * 4 for _ in range(TT)]
            pe_mm = [[0] * 4 for _ in range(TT)]
            st_out = []
            for t in range(TT):
                p = t % 2
                w_ = list(x_ready[t])
                if t >= 2:
                    w_.append(("sc", sc_ssqv[t - 2]))
                b.op("vector", "dv", lambda e, t=t, p=p: e.tensor_reduce(
                    out=rsum[:, p:p + 1], in_=xs[:, t, :], axis=AX.X,
                    op=ALU.add, negate=True), waits=w_, risky=True)
                dv_nm = b.op("vector", "dv", lambda e, p=p: e.tensor_scalar_mul(
                    out=nmean[:, p:p + 1], in0=rsum[:, p:p + 1], scalar1=INV_D))
                sc_ssqv[t] = b.op("scalar", "sc", lambda e, t=t, p=p: e.activation(
                    out=sq_scrap[:, p, :], in_=xs[:, t, :], func=AF.Square,
                    bias=nmean[:, p:p + 1], scale=1.0, accum_out=ssq[:, p:p + 1]),
                    waits=[("dv", dv_nm)] + ([("gp", gm)] if t == 0 else []),
                    chain=True)
                sc_sd = b.op("scalar", "sc", lambda e, p=p: e.activation(
                    out=sdvv[:, p:p + 1], in_=ssq[:, p:p + 1], func=AF.Sqrt,
                    bias=epsb[:, 0:1], scale=INV_D), chain=True)
                b.op("vector", "dv", lambda e, p=p: e.reciprocal(
                    out=rstd[:, p:p + 1], in_=sdvv[:, p:p + 1]), waits=[("sc", sc_sd)],
                    risky=True)
                b.op("vector", "dv", lambda e, t=t, p=p: e.tensor_scalar(
                    out=xh[:, p, :], in0=xs[:, t, :], scalar1=nmean[:, p:p + 1],
                    scalar2=rstd[:, p:p + 1], op0=ALU.add, op1=ALU.mult), waits=w_)
                b.op("vector", "dv", lambda e, p=p: e.tensor_reduce(
                    out=amax[:, p, :],
                    in_=xh[:, p, :].rearrange("p (c k) -> p c k", k=128),
                    axis=AX.X, op=ALU.max, apply_absolute_value=True), risky=True)
                b.op("vector", "dv", lambda e, p=p: e.reciprocal(
                    out=r2[:, p, :], in_=amax[:, p, :]), risky=True)
                b.op("vector", "dv", lambda e, p=p: e.tensor_scalar_mul(
                    out=r2[:, p, :], in0=r2[:, p, :], scalar1=224.0))
                dv_r2 = b.op("vector", "dv", lambda e, p=p: e.tensor_scalar_mul(
                    out=s2[:, p, :], in0=amax[:, p, :], scalar1=1.0 / 224.0))
                for blk in range(DC):
                    sc_q8[t] = b.op("vector", "dv", lambda e, p=p, blk=blk: e.tensor_scalar(
                        out=xq8[:, p, 128 * blk:128 * (blk + 1)],
                        in0=xh[:, p, 128 * blk:128 * (blk + 1)],
                        scalar1=r2[:, p, blk:blk + 1], scalar2=None, op0=ALU.mult))
                for blk in range(DC):
                    w_ = [("pe", pe_tr[t - 2][3])] if (t >= 2 and blk == 0) else []
                    dv_xdq[t] = b.op("vector", "dv", lambda e, p=p, blk=blk: e.tensor_scalar(
                        out=xdq[:, p, 128 * blk:128 * (blk + 1)],
                        in0=xq8[:, p, 128 * blk:128 * (blk + 1)],
                        scalar1=s2[:, p, blk:blk + 1], scalar2=None, op0=ALU.mult),
                        waits=w_)
                for k in range(4):
                    bank = 6 + (k % 2)
                    for j in range(4):
                        c = 4 * k + j
                        w_ = [("dv", dv_xdq[t])] if (k == 0 and j == 0) else []
                        if j == 0:
                            w_ += b.use_bank(bank)
                        if b.paranoid:
                            w_.append(("gp", gm))
                        pe_tr[t][k] = b.op("tensor", "pe", lambda e, p=p, c=c, bank=bank, j=j: e.transpose(
                            banks[bank][:, 128 * j:128 * (j + 1)],
                            xdq[:, p, 128 * c:128 * (c + 1)], ident_b[:]), waits=w_)
                    w_ev = [("pe", pe_tr[t][k])]
                    if t >= 2 and k == 0:
                        w_ev.append(("pe", pe_mm[t - 2][3]))
                    dv_ev[t][k] = b.op("vector", "dv", lambda e, k=k, p=p, bank=bank: e.tensor_copy(
                        out=xdT[:, p, 4 * k:4 * (k + 1), :],
                        in_=banks[bank][:, 0:512].rearrange("p (j k) -> p j k", k=128)),
                        waits=w_ev)
                    b.done_bank(bank, "dv", dv_ev[t][k])
                def emit_resid(tt):
                    for oc in range(4):
                        dv_res[tt][oc] = b.op("vector", "dv",
                                              lambda e, tt=tt, oc=oc: e.scalar_tensor_tensor(
                                                  out=xs[:, tt, 512 * oc:512 * (oc + 1)],
                                                  in0=ps[oc][:, 0:512], scalar=0.5,
                                                  in1=xs[:, tt, 512 * oc:512 * (oc + 1)],
                                                  op0=ALU.mult, op1=ALU.add),
                                              waits=[("pe", pe_mm[tt][oc])])
                        b.done_bank(oc, "dv", dv_res[tt][oc])
                    if store is not None:
                        st_out.append(b.dma("gpsimd", "gs",
                                            store[128 * tt:128 * (tt + 1), :],
                                            xs[:, tt, :], waits=[("dv", dv_res[tt][3])]))
                if t >= 1:
                    emit_resid(t - 1)
                for c in range(DC):
                    for oc in range(4):
                        w_ = []
                        if c == 0:
                            w_ = [("dv", dv_ev[t][3]), w_ready] + b.use_bank(oc)
                        pe_mm[t][oc] = b.op("tensor", "pe", lambda e, p=p, c=c, oc=oc: e.matmul(
                            ps[oc][:, 0:512], xdT[:, p, c, :],
                            wT[:, c, 512 * oc:512 * (oc + 1)],
                            start=(c == 0), stop=(c == DC - 1)), waits=w_)
            emit_resid(TT - 1)
            return dv_res, st_out

        dv_res1, _ = emit_ffn(x_str, wdeqT, ("dv", W_DONE),
                              [[("lx", 16 * TT)] for t in range(TT)], scrA)
        b.flush("ph01_wq_ffn1")
        ctxA0.close()
        zT_sb = ctxA.enter_context(nc.sbuf_tensor("zT_sb", [128, DC, 1024], BF16))

        pe_ztr = [[0] * 4 for _ in range(TT)]
        dv_zev = [[0] * 4 for _ in range(TT)]
        sc_ss2 = [0] * TT
        for t in range(TT):
            p = t % 2
            w_ = [("sc", sc_ss2[t - 2])] if t >= 2 else []
            b.op("vector", "dv", lambda e, t=t, p=p: e.tensor_reduce(
                out=rsum[:, p:p + 1], in_=x_str[:, t, :], axis=AX.X,
                op=ALU.add, negate=True), waits=w_, risky=True)
            dv_nm = b.op("vector", "dv", lambda e, p=p: e.tensor_scalar_mul(
                out=nmean[:, p:p + 1], in0=rsum[:, p:p + 1], scalar1=INV_D))
            sc_ss2[t] = b.op("scalar", "sc", lambda e, t=t, p=p: e.activation(
                out=sq_scrap[:, p, :], in_=x_str[:, t, :], func=AF.Square,
                bias=nmean[:, p:p + 1], scale=1.0, accum_out=ssq[:, p:p + 1]),
                waits=[("dv", dv_nm)], chain=True)
            sc_sd = b.op("scalar", "sc", lambda e, p=p: e.activation(
                out=sdvv[:, p:p + 1], in_=ssq[:, p:p + 1], func=AF.Sqrt,
                bias=epsb[:, 0:1], scale=INV_D), chain=True)
            b.op("vector", "dv", lambda e, p=p: e.reciprocal(
                out=rstd[:, p:p + 1], in_=sdvv[:, p:p + 1]), waits=[("sc", sc_sd)],
                risky=True)
            w_ = [("pe", pe_ztr[t - 2][3])] if t >= 2 else []
            dv_z = b.op("vector", "dv", lambda e, t=t, p=p: e.tensor_scalar(
                out=xh[:, p, :], in0=x_str[:, t, :], scalar1=nmean[:, p:p + 1],
                scalar2=rstd[:, p:p + 1], op0=ALU.add, op1=ALU.mult), waits=w_)
            for k in range(4):
                bank = 4 + (k % 2)
                for j in range(4):
                    c = 4 * k + j
                    w_ = [("dv", dv_z)] if (k == 0 and j == 0) else []
                    if t == 0 and k == 0 and j == 0:
                        w_.append(("gp", gm))
                    if j == 0:
                        w_ += b.use_bank(bank)
                    pe_ztr[t][k] = b.op("tensor", "pe", lambda e, p=p, c=c, bank=bank, j=j: e.transpose(
                        ps[bank][:, 128 * j:128 * (j + 1)],
                        xh[:, p, 128 * c:128 * (c + 1)], ident_f[:]), waits=w_)
                dv_zev[t][k] = b.op("vector", "dv", lambda e, t=t, k=k, bank=bank: e.tensor_copy(
                    out=zT_sb[:, 4 * k:4 * (k + 1), 128 * t:128 * (t + 1)],
                    in_=ps[bank][:, 0:512].rearrange("p (j k) -> p j k", k=128)),
                    waits=[("pe", pe_ztr[t][k])])
                b.done_bank(bank, "dv", dv_zev[t][k])
            b.dma("scalar", "sy", y_spill[128 * t:128 * (t + 1), :], x_str[:, t, :],
                  waits=[("dv", dv_res1[t][3])])
        SW_SPILL = 0
        for c in range(DC):
            SW_SPILL = b.dma("sync", "sw", wdeq_spill[c, :, :], wdeqT[:, c, :],
                             waits=[("dv", W_DONE)])
        cc_ag = [0, 0]
        for half in range(2):
            gs_z = 0
            for c in range(DC):
                gs_z = b.dma("gpsimd", "gs",
                             zag_in[half][128 * c:128 * (c + 1), :],
                             zT_sb[:, c, 512 * half:512 * (half + 1)],
                             waits=[("dv", dv_zev[4 * half + 3][3])])
            cc_ag[half] = b.op("gpsimd", "cc", lambda e, half=half: e.collective_compute(
                "AllGather", ALU.bypass, replica_groups=GROUPS,
                ins=[zag_in[half].ap().opt()], outs=[zag_out[half].ap().opt()]),
                waits=[("gs", gs_z)], chain=True)
        GS_SPILL = b.cnt["gs"]
        FENCE2 = [(s, b.cnt[s]) for s in ("sw", "sy", "gs", "pe", "dv", "sc")]
        b.flush("ph2_z_ag")

        ctxA.close()
        ctxBm = ExitStack()
        m_t = ctxBm.enter_context(nc.sbuf_tensor("m_t", [128, NH, S], BF16, side="right"))
        ctxB = ExitStack()
        q_t = ctxB.enter_context(nc.sbuf_tensor("q_t", [128, NH, S], BF16))
        k_t = ctxB.enter_context(nc.sbuf_tensor("k_t", [128, NH, S], BF16))
        v_t = ctxB.enter_context(nc.sbuf_tensor("v_t", [128, 32, 512], BF16))
        ctxB3 = ExitStack()
        zt_q = ctxB3.enter_context(nc.sbuf_tensor("zt_q", [128, DC, 512], BF16))
        wqf = ctxB3.enter_context(nc.sbuf_tensor("wqf", [128, 2, 512], F32))
        wq_b = ctxB3.enter_context(nc.sbuf_tensor("wq_b", [128, DC, 512], BF16))
        wk_b = ctxB3.enter_context(nc.sbuf_tensor("wk_b", [128, DC, 512], BF16))
        wv_b = ctxB3.enter_context(nc.sbuf_tensor("wv_b", [128, DC, 512], BF16))

        dv_w3 = 0
        nw = 0
        dv_wmat = {}
        for (wsrc, wbbuf) in ((wq_t, wq_b), (wk_t, wk_b), (wv_t, wv_b)):
            for c in range(DC):
                p = nw % 2
                ldw = b.dma("scalar", f"lw3{p}", wqf[:, p, :],
                            wsrc[128 * c:128 * (c + 1), :],
                            waits=([("dv", dv_w3)] if nw >= 2 else FENCE2))
                dv_w3 = b.op("vector", "dv",
                             lambda e, p=p, c=c, wbbuf=wbbuf: e.tensor_copy(
                                 out=wbbuf[:, c, :], in_=wqf[:, p, :]),
                             waits=[(f"lw3{p}", ldw)])
                nw += 1
            dv_wmat[id(wbbuf)] = dv_w3
        pe_zread = [0] * DC
        for hq in range(8):
            qq, half = divmod(hq, 2)
            ldz = 0
            for c in range(DC):
                w_ = [("pe", pe_zread[c])] if hq >= 1 else list(FENCE2)
                ldz = b.dma("sync", f"lz{hq % 2}", zt_q[:, c, :],
                            zag_out[half][D * qq + 128 * c:D * qq + 128 * (c + 1), :],
                            waits=w_ + [("cc", cc_ag[half])])
            zq_ready = (f"lz{hq % 2}", ldz)
            for (wb, dst) in ((wq_b, q_t), (wk_b, k_t)):
                pe_h = [0] * NH
                for c in range(DC):
                    for h in range(NH):
                        w_ = []
                        if c == 0:
                            w_ = [zq_ready, ("dv", dv_wmat[id(wb)])] + b.use_bank(h)
                        pe_h[h] = b.op("tensor", "pe",
                                       lambda e, c=c, h=h, wb=wb: e.matmul(
                                           ps[h][:, 0:512],
                                           wb[:, c, 128 * h:128 * (h + 1)],
                                           zt_q[:, c, :],
                                           start=(c == 0), stop=(c == DC - 1)),
                                       waits=w_)
                for h in range(NH):
                    dvx = b.op("vector", "dv",
                               lambda e, h=h, hq=hq, dst=dst: e.tensor_copy(
                                   out=dst[:, h, 512 * hq:512 * (hq + 1)],
                                   in_=ps[h][:, 0:512]), waits=[("pe", pe_h[h])])
                    b.done_bank(h, "dv", dvx)
            pe_g = [0] * 4
            for c in range(DC):
                for u in range(4):
                    w_ = []
                    if c == 0:
                        w_ = [zq_ready, ("dv", dv_w3)] + b.use_bank(u)
                    pe_g[u] = b.op("tensor", "pe",
                                   lambda e, c=c, u=u: e.matmul(
                                       ps[u][:, 0:512],
                                       zt_q[:, c, 128 * u:128 * (u + 1)],
                                       wv_b[:, c, 0:512],
                                       start=(c == 0), stop=(c == DC - 1)), waits=w_)
                    pe_zread[c] = pe_g[u]
            for u in range(4):
                dvx = b.op("vector", "dv", lambda e, hq=hq, u=u: e.tensor_copy(
                    out=v_t[:, 4 * hq + u, :], in_=ps[u][:, 0:512]),
                    waits=[("pe", pe_g[u])])
                b.done_bank(u, "dv", dvx)
        PH3_DV = b.cnt["dv"]
        b.flush("ph3_qkv")
        ctxB3.close()

        ctxB4 = ExitStack()
        gate_h = ctxB4.enter_context(nc.sbuf_tensor("gate_h", [128, 2, S], BF16))
        av_sb = ctxB4.enter_context(nc.sbuf_tensor("av_sb", [128, 1, S], BF16))
        p_t = ctxB4.enter_context(nc.sbuf_tensor("p_t", [128, 3, 1024], BF16))
        cs_row = ctxB4.enter_context(nc.sbuf_tensor("cs_row", [1, S], F32))
        cs128 = ctxB4.enter_context(nc.sbuf_tensor("cs128", [128, 32], F32))
        rc128 = ctxB4.enter_context(nc.sbuf_tensor("rc128", [128, 32], F32))
        recip_b = ctxB4.enter_context(nc.sbuf_tensor("recip_b", [128, S], F32))

        dv_mt = [0] * NH
        pt_pe = [0, 0, 0]
        for h in range(NH):
            ph = h % 2
            scg = 0
            for qc in range(NQC):
                bank = qc % 4
                w_ = b.use_bank(bank)
                if qc == 0:
                    w_ += [("gp", gm), ("dv", GWT_B)]
                    if h == 0:
                        w_.append(("dv", PH3_DV))
                    if h >= 2:
                        w_.append(("dv", dv_mt[h - 2]))
                peg = b.op("tensor", "pe", lambda e, h=h, qc=qc, bank=bank: e.matmul(
                    ps[bank][:, 0:512], gwt_b[:], q_t[:, h, 512 * qc:512 * (qc + 1)],
                    start=True, stop=True), waits=w_)
                scg = b.op("scalar", "sc", lambda e, ph=ph, qc=qc, bank=bank: e.activation(
                    out=gate_h[:, ph, 512 * qc:512 * (qc + 1)], in_=ps[bank][:, 0:512],
                    func=AF.Sigmoid, bias=gb_sb[:, 0:1], scale=1.0),
                    waits=[("pe", peg)] + ([("ld", 32), ("gp", gm)]
                                           if (h == 0 and qc == 0) else []),
                    chain=True)
                b.done_bank(bank, "sc", scg)
            dva = 0
            dvc = 0
            for qc in range(NQC):
                klim = 4 * qc + 4
                npair = klim // 2
                pend = []

                def flush_avcs(h=h):
                    pr, slot, pslot, sc_e = pend.pop(0)
                    ki0, ki1 = 2 * pr, 2 * pr + 1
                    klim_ = pend_klim[0]
                    pe_av = pe_cs = 0
                    for u, ki in ((0, ki0), (1, ki1)):
                        w_ = [("sc", sc_e)]
                        if ki == 0:
                            w_ += b.use_bank(4)
                        pe_av = b.op("tensor", "pe",
                                     lambda e, h=h, ki=ki, pslot=pslot, u=u, klim_=klim_: e.matmul(
                                         ps[4][:, 0:512],
                                         v_t[:, ki, 128 * h:128 * (h + 1)],
                                         p_t[:, pslot, 512 * u:512 * (u + 1)],
                                         start=(ki == 0), stop=(ki == klim_ - 1)),
                                     waits=w_)
                    for u, ki in ((0, ki0), (1, ki1)):
                        w_ = [("sc", sc_e)]
                        if ki == 0:
                            w_ += b.use_bank(5)
                        pe_cs = b.op("tensor", "pe",
                                     lambda e, ki=ki, pslot=pslot, u=u, klim_=klim_: e.matmul(
                                         ps[5][0:1, 0:512], ones_b[:],
                                         p_t[:, pslot, 512 * u:512 * (u + 1)],
                                         start=(ki == 0), stop=(ki == klim_ - 1)),
                                     waits=w_)
                    pt_pe[pslot] = pe_cs
                    return pe_av, pe_cs

                pend_klim = [klim]
                pe_av_last = pe_cs_last = 0
                for pr in range(npair):
                    ki0, ki1 = 2 * pr, 2 * pr + 1
                    off0 = max(0, 128 * ki0 - 512 * qc)
                    off1 = max(0, 128 * ki1 - 512 * qc)
                    slot = pr % 2
                    pslot = pr % 3
                    wS = b.use_bank(2 * slot) + b.use_bank(2 * slot + 1)
                    pes = b.op("tensor", "pe",
                               lambda e, h=h, qc=qc, ki0=ki0, off0=off0, slot=slot: e.matmul(
                                   P[slot][:, off0:512],
                                   k_t[:, h, 128 * ki0:128 * (ki0 + 1)],
                                   q_t[:, h, 512 * qc + off0:512 * (qc + 1)],
                                   start=True, stop=True), waits=wS)
                    pes = b.op("tensor", "pe",
                               lambda e, h=h, qc=qc, ki1=ki1, off1=off1, slot=slot: e.matmul(
                                   P[slot][:, 512 + off1:1024],
                                   k_t[:, h, 128 * ki1:128 * (ki1 + 1)],
                                   q_t[:, h, 512 * qc + off1:512 * (qc + 1)],
                                   start=True, stop=True), waits=wS)
                    dv_ms = 0
                    if ki1 >= 4 * qc:
                        wm = [("pe", pes)]
                        if h == 0 and qc == 0 and pr == 0:
                            wm.append(("gp", gm))
                        if off0 or ki0 >= 4 * qc:
                            dv_ms = b.op("vector", "dv",
                                         lambda e, slot=slot, off0=off0: e.tensor_tensor(
                                             out=P[slot][:, off0:off0 + 128],
                                             in0=P[slot][:, off0:off0 + 128],
                                             in1=mask_add[:], op=ALU.add), waits=wm)
                            wm = []
                        dv_ms = b.op("vector", "dv",
                                     lambda e, slot=slot, off1=off1: e.tensor_tensor(
                                         out=P[slot][:, 512 + off1:512 + off1 + 128],
                                         in0=P[slot][:, 512 + off1:512 + off1 + 128],
                                         in1=mask_add[:], op=ALU.add), waits=wm)
                        if off0:
                            dv_ms = b.op("vector", "dv", lambda e, slot=slot, off0=off0: e.memset(
                                P[slot][:, 0:off0], -1.0e30))
                        if off1:
                            dv_ms = b.op("vector", "dv", lambda e, slot=slot, off1=off1: e.memset(
                                P[slot][:, 512:512 + off1], -1.0e30))
                    w_e = [("pe", pes), ("pe", pt_pe[pslot]), ("dv", dv_ms)]
                    sc_e = b.op("scalar", "sc",
                                lambda e, slot=slot, pslot=pslot: e.activation(
                                    out=p_t[:, pslot, :], in_=P[slot][:, 0:1024],
                                    func=AF.Exp, bias=negC[:, 0:1], scale=SCALE),
                                chain=True, waits=w_e)
                    b.done_bank(2 * slot, "sc", sc_e)
                    b.done_bank(2 * slot + 1, "sc", sc_e)
                    pend.append((pr, slot, pslot, sc_e))
                    if len(pend) > 2:
                        pe_av_last, pe_cs_last = flush_avcs()
                while pend:
                    pe_av_last, pe_cs_last = flush_avcs()
                dva = b.op("vector", "dv", lambda e, ph=ph, qc=qc: e.tensor_copy(
                    out=av_sb[:, 0, 512 * qc:512 * (qc + 1)], in_=ps[4][:, 0:512]),
                    waits=[("pe", pe_av_last)])
                b.done_bank(4, "dv", dva)
                dvc = b.op("vector", "dv", lambda e, qc=qc: e.tensor_copy(
                    out=cs_row[0:1, 512 * qc:512 * (qc + 1)], in_=ps[5][0:1, 0:512]),
                    waits=[("pe", pe_cs_last)])
                b.done_bank(5, "dv", dvc)
            ld1 = b.dma("sync", "lcs", cs128[:, :], cs_row[0:1, :], waits=[("dv", dvc)])
            dvr = b.op("vector", "dv", lambda e: e.reciprocal(out=rc128[:, :], in_=cs128[:, :]),
                       waits=[("lcs", ld1)], risky=True)
            ld2 = b.dma("sync", "lcs", cs_row[0:1, :], rc128[:, :], waits=[("dv", dvr)])
            dv_rb = 0
            for qc in range(NQC):
                bank = qc % 4
                w_ = b.use_bank(bank) + [("lcs", ld2)]
                peb = b.op("tensor", "pe", lambda e, qc=qc, bank=bank: e.matmul(
                    ps[bank][:, 0:512], ones1f[:],
                    cs_row[0:1, 512 * qc:512 * (qc + 1)],
                    start=True, stop=True), waits=w_)
                dv_rb = b.op("vector", "dv", lambda e, qc=qc, bank=bank: e.tensor_copy(
                    out=recip_b[:, 512 * qc:512 * (qc + 1)], in_=ps[bank][:, 0:512]),
                    waits=[("pe", peb)])
                b.done_bank(bank, "dv", dv_rb)
            dvt = b.op("vector", "dv", lambda e, ph=ph: e.tensor_tensor(
                out=recip_b[:, :], in0=av_sb[:, 0, :], in1=recip_b[:, :], op=ALU.mult),
                waits=[("dv", dv_rb), ("sc", scg)])
            dv_mt[h] = b.op("vector", "dv", lambda e, h=h, ph=ph: e.tensor_tensor(
                out=m_t[:, h, :], in0=recip_b[:, :], in1=gate_h[:, ph, :], op=ALU.mult),
                waits=[("dv", dvt)])
        b.flush("ph4_attn")
        ctxB4.close()
        ctxB.close()

        ctxC = ExitStack()
        x_str2 = ctxC.enter_context(nc.sbuf_tensor("x_str2", [128, TT, D], F32))
        wdeqT2 = ctxC.enter_context(nc.sbuf_tensor("wdeqT2", [128, DC, D], BF16))
        ctxC0 = ExitStack()
        wof = ctxC0.enter_context(nc.sbuf_tensor("wof", [128, 1, D], F32))
        wo_b = ctxC0.enter_context(nc.sbuf_tensor("wo_b", [128, 4, D], BF16))
        rs_buf = ctxC0.enter_context(nc.sbuf_tensor("rs_buf", [128, 2, QCW], BF16))

        dv_wo = 0
        for c in range(4):
            ldw = b.dma("sync", "lwo", wof[:, 0, :], wo_t[128 * c:128 * (c + 1), :],
                        waits=([("dv", dv_wo)] if c >= 1 else [("dv", dv_mt[NH - 1])]))
            dv_wo = b.op("vector", "dv", lambda e, c=c: e.tensor_copy(
                out=wo_b[:, c, :], in_=wof[:, 0, :]), waits=[("lwo", ldw)])
        cc_rs = [0] * 4
        gr_last = [0, 0]
        for e2p in range(4):
            for qt in range(32):
                bank = qt % 2
                peo = 0
                for c in range(4):
                    w_ = []
                    if c == 0:
                        w_ = [("dv", dv_wo), ("dv", dv_mt[NH - 1]),
                              ("dv", dv_mt[NH - 2])] + b.use_bank(bank)
                    peo = b.op("tensor", "pe",
                               lambda e, e2p=e2p, qt=qt, c=c, bank=bank: e.matmul(
                                   ps[bank][:, 0:512], m_t[:, c, 128 * qt:128 * (qt + 1)],
                                   wo_b[:, c, 512 * e2p:512 * (e2p + 1)],
                                   start=(c == 0), stop=(c == 3)), waits=w_)
                dvo = b.op("vector", "dv", lambda e, bank=bank: e.tensor_copy(
                    out=rs_buf[:, bank, :], in_=ps[bank][:, 0:512]),
                    waits=[("pe", peo),
                           (f"gr{bank}", gr_last[bank])])
                b.done_bank(bank, "dv", dvo)
                gr_last[bank] = b.dma("gpsimd", f"gr{bank}",
                                      rs_in[e2p][128 * qt:128 * (qt + 1), :],
                                      rs_buf[:, bank, :], waits=[("dv", dvo)])
            cc_rs[e2p] = b.op("gpsimd", "cc", lambda e, e2p=e2p: e.collective_compute(
                "ReduceScatter", ALU.add, replica_groups=GROUPS,
                ins=[rs_in[e2p].ap().opt()], outs=[rs_out[e2p].ap().opt()]),
                waits=[("gr0", gr_last[0]), ("gr1", gr_last[1])], chain=True)
        if taps:
            gt = b.dma("gpsimd", "gs", dbg_y[:, :], y_spill[:, :],
                       waits=[("gs", GS_SPILL)])
            gt = b.dma("gpsimd", "gs", dbg_wd[:, :, :], wdeq_spill[:, :, :])
            gt = b.dma("gpsimd", "gs", dbg_zo[:, 0:512], zag_out[0][:, :],
                       waits=[("cc", cc_ag[1])])
            gt = b.dma("gpsimd", "gs", dbg_zo[:, 512:1024], zag_out[1][:, :])
            gt = b.dma("gpsimd", "gs", dbg_rs[:, :], rs_out[0][:, :],
                       waits=[("cc", cc_rs[3])])
            b.wait("gpsimd", [("gs", gt)])
        b.flush("ph5_oproj_rs")
        ctxC0.close()
        ctxBm.close()
        y_buf = ctxC.enter_context(nc.sbuf_tensor("y_buf", [128, 2, D], F32))
        rs_sb = ctxC.enter_context(nc.sbuf_tensor("rs_sb", [128, 2, 4, QCW], BF16))
        xh2 = ctxC.enter_context(nc.sbuf_tensor("xh2", [128, 2, D], F32))
        xq82 = ctxC.enter_context(nc.sbuf_tensor("xq82", [128, 2, D], FP8))
        xdq2 = ctxC.enter_context(nc.sbuf_tensor("xdq2", [128, 2, D], BF16))
        xdT2 = ctxC.enter_context(nc.sbuf_tensor("xdT2", [128, 2, DC, 128], BF16))
        scrC = dict(xh=xh2, xq8=xq82, xdq=xdq2, xdT=xdT2, sq=sq_scrap)

        ld_wd = 0
        for c in range(DC):
            ld_wd = b.dma("sync", "lwd", wdeqT2[:, c, :], wdeq_spill[c, :, :],
                          waits=[("sw", SW_SPILL), ("dv", dv_mt[NH - 1])])
        dv_x2 = [0] * TT
        for t in range(TT):
            p = t % 2
            w_ = [("dv", dv_x2[t - 2])] if t >= 2 else []
            ldy = b.dma("sync", f"ly{p}", y_buf[:, p, :],
                        y_spill[128 * t:128 * (t + 1), :],
                        waits=w_ + [("sy", 16 * TT), ("dv", dv_mt[NH - 1])])
            ldr = 0
            for pp in range(4):
                ldr = b.dma("sync", f"lr{p}", rs_sb[:, p, pp, :],
                            rs_out[pp][128 * t:128 * (t + 1), :],
                            waits=w_ + [("cc", cc_rs[pp])])
            for pp in range(4):
                dv_x2[t] = b.op("vector", "dv", lambda e, t=t, p=p, pp=pp: e.tensor_tensor(
                    out=x_str2[:, t, 512 * pp:512 * (pp + 1)],
                    in0=y_buf[:, p, 512 * pp:512 * (pp + 1)],
                    in1=rs_sb[:, p, pp, :], op=ALU.add),
                    waits=[(f"ly{p}", ldy), (f"lr{p}", ldr)])
        b.flush("ph6_resid")

        _, st_out = emit_ffn(x_str2, wdeqT2, ("lwd", ld_wd),
                             [[("dv", dv_x2[t])] for t in range(TT)], scrC,
                             store=out_sh)
        b.wait("gpsimd", [("gs", st_out[-1])])
        b.flush("ph7_ffn2")
        ctxC.close()

    return nc


_cache = {}


def kernel(L0_attn_x, L0_attn_q_w, L0_attn_k_w, L0_attn_v_w, L0_attn_o_w,
           L0_attn_gate_w, L0_attn_gate_b, L0_ffn_weight):
    x = np.ascontiguousarray(np.asarray(L0_attn_x, dtype=np.float32).reshape(8192, 2048))
    qw = np.asarray(L0_attn_q_w, dtype=np.float32)
    kw = np.asarray(L0_attn_k_w, dtype=np.float32)
    vw = np.asarray(L0_attn_v_w, dtype=np.float32)
    ow = np.asarray(L0_attn_o_w, dtype=np.float32)
    gw = np.asarray(L0_attn_gate_w, dtype=np.float32)
    gbv = np.ascontiguousarray(np.asarray(L0_attn_gate_b, dtype=np.float32).reshape(128, 1))
    fw = np.ascontiguousarray(np.asarray(L0_ffn_weight, dtype=np.float32))
    owt = np.ascontiguousarray(ow.T)
    gwt_host = np.ascontiguousarray(gw.T)

    if "nc" not in _cache:
        _cache["nc"] = build()
    nc = _cache["nc"]

    in_maps = []
    for c in range(NCORES):
        bb, r = divmod(c, 4)
        sl = slice(512 * r, 512 * (r + 1))
        in_maps.append({
            "x_sh": np.ascontiguousarray(x[4096 * bb + 1024 * r: 4096 * bb + 1024 * (r + 1)]),
            "wq_t": np.ascontiguousarray(qw[sl, :].T),
            "wk_t": np.ascontiguousarray(kw[sl, :].T),
            "wv_t": np.ascontiguousarray(vw[sl, :].T),
            "wo_t": np.ascontiguousarray(owt[sl, :]),
            "gwt": gwt_host,
            "gb": gbv,
            "wf": fw,
        })
    res = run_bass_kernel_spmd(nc, in_maps, core_ids=list(range(NCORES)))
    out = np.concatenate([np.asarray(res.results[c]["out_sh"]) for c in range(NCORES)],
                         axis=0)
    return out.reshape(2, 4096, 2048).astype(np.float32)


# version 06
# speedup vs baseline: 1.4177x; 1.0070x over previous
import numpy as np
from contextlib import ExitStack

import concourse.bass as bass
import concourse.mybir as mybir
from concourse.bass_utils import run_bass_kernel_spmd

F32 = mybir.dt.float32
BF16 = mybir.dt.bfloat16
FP8 = mybir.dt.float8e4
AF = mybir.ActivationFunctionType
ALU = mybir.AluOpType
AX = mybir.AxisListType

NCORES = 8
TT = 8
D = 2048
DC = 16
S = 4096
NH = 4
QCW = 512
NQC = S // QCW
C_SHIFT = 20.0
SCALE = 0.08838834764831845
INV_D = 1.0 / 2048.0
EPS = 1e-5
GROUPS = [[0, 1, 2, 3], [4, 5, 6, 7]]


class Builder:
    def __init__(self, nc, paranoid=False):
        self.nc = nc
        self.paranoid = paranoid
        self.q = {k: [] for k in ("sync", "gpsimd", "vector", "scalar", "tensor")}
        self.sem = {}
        self.cnt = {}
        self.bank_state = [None] * 8
        self.risky = {}

    def add_sem(self, ctx, name):
        self.sem[name] = ctx.enter_context(self.nc.semaphore(name))
        self.cnt[name] = 0

    def flush(self, blockname):
        with self.nc.Block(blockname) as block:
            for eng, fns in self.q.items():
                if fns:
                    getattr(block, eng)(lambda e, fns=tuple(fns): [f(e) for f in fns])
        self.q = {k: [] for k in self.q}

    def op(self, eng, semname, fn, waits=(), inc=1, chain=False, risky=False):
        w = [(s, v) for (s, v) in waits if v > 0]
        if chain or (self.paranoid and eng == "vector"):
            prev = self.cnt[semname]
            if prev > 0:
                w.append((semname, prev))
        pend = self.risky.pop(eng, None)
        if pend is not None:
            w.append(pend)
        self.cnt[semname] += inc
        if risky:
            self.risky[eng] = (semname, self.cnt[semname])
        sems = self.sem

        def run(e, w=tuple(w), fn=fn, semname=semname, inc=inc):
            seen = {}
            for (s, v) in w:
                if seen.get(s, -1) < v:
                    e.wait_ge(sems[s], v)
                    seen[s] = v
            fn(e).then_inc(sems[semname], inc)
        self.q[eng].append(run)
        return self.cnt[semname]

    def wait(self, eng, waits):
        w = [(s, v) for (s, v) in waits if v > 0]
        sems = self.sem

        def run(e):
            for (s, v) in w:
                e.wait_ge(sems[s], v)
        self.q[eng].append(run)

    def dma(self, eng, semname, out, in_, waits=()):
        return self.op(eng, semname, lambda e: e.dma_start(out=out, in_=in_),
                       waits=waits, inc=16)

    def use_bank(self, k):
        st = self.bank_state[k]
        return [st] if st is not None else []

    def done_bank(self, k, semname, val):
        self.bank_state[k] = (semname, val)


def build(paranoid=False, taps=False):
    nc = bass.Bass("TRN2", target_bir_lowering=False)

    x_sh = nc.dram_tensor("x_sh", [1024, D], F32, kind="ExternalInput")
    wq_t = nc.dram_tensor("wq_t", [D, 512], F32, kind="ExternalInput")
    wk_t = nc.dram_tensor("wk_t", [D, 512], F32, kind="ExternalInput")
    wv_t = nc.dram_tensor("wv_t", [D, 512], F32, kind="ExternalInput")
    wo_t = nc.dram_tensor("wo_t", [512, D], F32, kind="ExternalInput")
    gwt = nc.dram_tensor("gwt", [128, 128], F32, kind="ExternalInput")
    gb = nc.dram_tensor("gb", [128, 1], F32, kind="ExternalInput")
    wf = nc.dram_tensor("wf", [D, D], F32, kind="ExternalInput")
    out_sh = nc.dram_tensor("out_sh", [1024, D], F32, kind="ExternalOutput")

    zag_in = [nc.dram_tensor(f"zag_in{i}", [D, QCW], BF16) for i in range(2)]
    zag_out = [nc.dram_tensor(f"zag_out{i}", [4 * D, QCW], BF16) for i in range(2)]
    rs_in = [nc.dram_tensor(f"rs_in{p}", [4 * 128, D], BF16) for p in range(8)]
    rs_out = [nc.dram_tensor(f"rs_out{p}", [128, D], BF16) for p in range(8)]
    y_spill = nc.dram_tensor("y_spill", [1024, D], F32)
    wdeq_spill = nc.dram_tensor("wdeq_spill", [DC, 128, D], BF16)
    if taps:
        dbg_am = nc.dram_tensor("dbg_am", [DC, 128, DC], F32, kind="ExternalOutput")
        dbg_r2 = nc.dram_tensor("dbg_r2", [DC, 128, DC], F32, kind="ExternalOutput")
        dbg_q8 = nc.dram_tensor("dbg_q8", [DC, 128, D], FP8, kind="ExternalOutput")
        dbg_dq = nc.dram_tensor("dbg_dq", [DC, 128, D], BF16, kind="ExternalOutput")
        dbg_y = nc.dram_tensor("dbg_y", [1024, D], F32, kind="ExternalOutput")
        dbg_wd = nc.dram_tensor("dbg_wd", [DC, 128, D], BF16, kind="ExternalOutput")
        dbg_zo = nc.dram_tensor("dbg_zo", [4 * D, 1024], BF16, kind="ExternalOutput")
        dbg_q = nc.dram_tensor("dbg_q", [128, NH * S], BF16, kind="ExternalOutput")
        dbg_k = nc.dram_tensor("dbg_k", [128, NH * S], BF16, kind="ExternalOutput")
        dbg_v = nc.dram_tensor("dbg_v", [128, 32 * 512], BF16, kind="ExternalOutput")
        dbg_m = nc.dram_tensor("dbg_m", [128, NH * S], BF16, kind="ExternalOutput")
        dbg_g = nc.dram_tensor("dbg_g", [128, 2 * S], BF16, kind="ExternalOutput")
        dbg_rs = nc.dram_tensor("dbg_rs", [1024, QCW], BF16, kind="ExternalOutput")

    b = Builder(nc, paranoid=paranoid)
    with ExitStack() as ctx:
        for s in ("ld", "gs", "cc", "pe", "dv", "sc", "gp", "lx",
                  "lwf0", "lwf1", "lz0", "lz1", "lw30", "lw31", "lcs",
                  "lwo", "lwd", "ly0", "ly1", "lr0", "lr1", "gr0", "gr1",
                  "sy", "sw", "vy", "ts"):
            b.add_sem(ctx, s)

        P = [ctx.enter_context(nc.psum_tensor(f"P{k}", [128, 1024], F32))
             for k in range(3)]
        pt = [ctx.enter_context(nc.psum_tensor(f"pt{k}", [128, 1024], BF16))
              for k in range(2)]

        class PV:
            def __init__(self, t, off):
                self.t = t
                self.off = off

            def __getitem__(self, idx):
                rows, cols = idx
                start = self.off + (cols.start or 0)
                stop = self.off + (cols.stop if cols.stop is not None else QCW)
                return self.t[rows, start:stop]

        ps = [PV(P[k // 2], QCW * (k % 2)) for k in range(6)]
        banks = ps + pt

        ident_f = ctx.enter_context(nc.sbuf_tensor("ident_f", [128, 128], F32))
        ident_b = ctx.enter_context(nc.sbuf_tensor("ident_b", [128, 128], BF16))
        mask_add = ctx.enter_context(nc.sbuf_tensor("mask_add", [128, 128], F32))
        ones_b = ctx.enter_context(nc.sbuf_tensor("ones_b", [128, 1], BF16))
        ones1f = ctx.enter_context(nc.sbuf_tensor("ones1f", [1, 128], F32))
        negC = ctx.enter_context(nc.sbuf_tensor("negC", [128, 1], F32))
        epsb = ctx.enter_context(nc.sbuf_tensor("epsb", [128, 1], F32))
        gb_sb = ctx.enter_context(nc.sbuf_tensor("gb_sb", [128, 1], F32))
        gwt_f = ctx.enter_context(nc.sbuf_tensor("gwt_f", [128, 128], F32))
        gwt_b = ctx.enter_context(nc.sbuf_tensor("gwt_b", [128, 128], BF16))
        rsum = ctx.enter_context(nc.sbuf_tensor("rsum", [128, 2], F32))
        nmean = ctx.enter_context(nc.sbuf_tensor("nmean", [128, 2], F32))
        ssq = ctx.enter_context(nc.sbuf_tensor("ssq", [128, 2], F32))
        sdvv = ctx.enter_context(nc.sbuf_tensor("sdvv", [128, 2], F32))
        rstd = ctx.enter_context(nc.sbuf_tensor("rstd", [128, 2], F32))
        amax = ctx.enter_context(nc.sbuf_tensor("amax", [128, 2, DC], F32))
        r2 = ctx.enter_context(nc.sbuf_tensor("r2", [128, 2, DC], F32))
        s2 = ctx.enter_context(nc.sbuf_tensor("s2", [128, 2, DC], F32))
        sq_scrap = ctx.enter_context(nc.sbuf_tensor("sq_scrap", [128, 2, D], BF16))

        for (tname, val) in ((ident_f, 0.0), (ident_b, 0.0), (mask_add, 0.0),
                             (ones_b, 1.0), (ones1f, 1.0), (negC, -C_SHIFT),
                             (epsb, EPS)):
            b.op("gpsimd", "gp", lambda e, t=tname, v=val: e.memset(t[:], v), chain=True)
        b.op("gpsimd", "gp", lambda e: e.affine_select(
            out=ident_f[:], in_=ident_f[:], compare_op=ALU.not_equal,
            fill=1.0, base=0, pattern=[[-1, 128]], channel_multiplier=1), chain=True)
        b.op("gpsimd", "gp", lambda e: e.affine_select(
            out=ident_b[:], in_=ident_b[:], compare_op=ALU.not_equal,
            fill=1.0, base=0, pattern=[[-1, 128]], channel_multiplier=1), chain=True)
        gm = b.op("gpsimd", "gp", lambda e: e.affine_select(
            out=mask_add[:], in_=mask_add[:], compare_op=ALU.is_ge,
            fill=-1.0e30, base=0, pattern=[[1, 128]], channel_multiplier=-1), chain=True)
        b.dma("sync", "ld", gb_sb[:], gb[:, :])
        lgw = b.dma("sync", "ld", gwt_f[:], gwt[:, :])
        GWT_B = b.op("vector", "dv", lambda e: e.tensor_copy(out=gwt_b[:], in_=gwt_f[:]),
                     waits=[("ld", lgw)])

        ctxA = ExitStack()
        x_str = ctxA.enter_context(nc.sbuf_tensor("x_str", [128, TT, D], F32))
        wdeqT = ctxA.enter_context(nc.sbuf_tensor("wdeqT", [128, DC, D], BF16))
        xh = ctxA.enter_context(nc.sbuf_tensor("xh", [128, 2, D], F32))
        xq8 = ctxA.enter_context(nc.sbuf_tensor("xq8", [128, 2, D], FP8))
        xdq = ctxA.enter_context(nc.sbuf_tensor("xdq", [128, 2, D], BF16))
        xdT = ctxA.enter_context(nc.sbuf_tensor("xdT", [128, 2, DC, 128], BF16))
        scrA = dict(xh=xh, xq8=xq8, xdq=xdq, xdT=xdT, sq=sq_scrap)
        ctxA0 = ExitStack()
        wfn = ctxA0.enter_context(nc.sbuf_tensor("wfn", [128, 2, D], F32))
        wq8 = ctxA0.enter_context(nc.sbuf_tensor("wq8", [128, 2, D], FP8))
        wdq = ctxA0.enter_context(nc.sbuf_tensor("wdq", [128, 2, D], BF16))

        ld_x = [b.dma("sync", "lx", x_str[:, t, :], x_sh[128 * t:128 * (t + 1), :])
                for t in range(TT)]

        sc_wq8 = [0] * DC
        dv_wdq = [0] * DC
        pe_wtr = [[0] * 4 for _ in range(DC)]
        GP_INIT = gm
        for i in range(DC):
            p = i % 2
            ldwf = b.dma("sync", f"lwf{p}", wfn[:, p, :], wf[128 * i:128 * (i + 1), :],
                         waits=([("dv", sc_wq8[i - 2])] if i >= 2 else []))
            b.op("vector", "dv", lambda e, p=p: e.tensor_reduce(
                out=amax[:, p, :],
                in_=wfn[:, p, :].rearrange("p (c k) -> p c k", k=128),
                axis=AX.X, op=ALU.max, apply_absolute_value=True),
                waits=[(f"lwf{p}", ldwf)], risky=True)
            b.op("vector", "dv", lambda e, p=p: e.reciprocal(
                out=r2[:, p, :], in_=amax[:, p, :]), risky=True)
            b.op("vector", "dv", lambda e, p=p: e.tensor_scalar_mul(
                out=r2[:, p, :], in0=r2[:, p, :], scalar1=224.0))
            dv_r2w = b.op("vector", "dv", lambda e, p=p: e.tensor_scalar_mul(
                out=s2[:, p, :], in0=amax[:, p, :], scalar1=1.0 / 224.0))
            for blk in range(DC):
                sc_wq8[i] = b.op("vector", "dv", lambda e, p=p, blk=blk: e.tensor_scalar(
                    out=wq8[:, p, 128 * blk:128 * (blk + 1)],
                    in0=wfn[:, p, 128 * blk:128 * (blk + 1)],
                    scalar1=r2[:, p, blk:blk + 1], scalar2=None, op0=ALU.mult))
            for blk in range(DC):
                w_ = [("pe", pe_wtr[i - 2][3])] if (i >= 2 and blk == 0) else []
                dv_wdq[i] = b.op("vector", "dv", lambda e, p=p, blk=blk: e.tensor_scalar(
                    out=wdq[:, p, 128 * blk:128 * (blk + 1)],
                    in0=wq8[:, p, 128 * blk:128 * (blk + 1)],
                    scalar1=s2[:, p, blk:blk + 1], scalar2=None, op0=ALU.mult),
                    waits=w_)
            if taps:
                b.dma("gpsimd", "gs", dbg_am[i, :, :], amax[:, p, :],
                      waits=[("dv", dv_wdq[i])])
                b.dma("gpsimd", "gs", dbg_r2[i, :, :], r2[:, p, :])
                b.dma("gpsimd", "gs", dbg_q8[i, :, :], wq8[:, p, :])
                b.dma("gpsimd", "gs", dbg_dq[i, :, :], wdq[:, p, :])
            for k in range(4):
                bank = 6 + (k % 2)
                for j in range(4):
                    c = 4 * k + j
                    w_ = [("dv", dv_wdq[i])] if (k == 0 and j == 0) else []
                    if j == 0:
                        w_ += b.use_bank(bank)
                    if (i == 0 and k == 0 and j == 0) or b.paranoid:
                        w_.append(("gp", GP_INIT))
                    pe_wtr[i][k] = b.op("tensor", "pe", lambda e, p=p, c=c, bank=bank, j=j: e.transpose(
                        banks[bank][:, 128 * j:128 * (j + 1)],
                        wdq[:, p, 128 * c:128 * (c + 1)], ident_b[:]),
                        waits=w_)
                dvev = b.op("vector", "dv", lambda e, i=i, k=k, bank=bank: e.tensor_copy(
                    out=wdeqT[:, 4 * k:4 * (k + 1), 128 * i:128 * (i + 1)],
                    in_=banks[bank][:, 0:512].rearrange("p (j k) -> p j k", k=128)),
                    waits=[("pe", pe_wtr[i][k])])
                b.done_bank(bank, "dv", dvev)
        W_DONE = dvev

        def emit_ffn(xs, wT, w_ready, x_ready, scr, store=None):
            xh = scr["xh"]; xq8 = scr["xq8"]; xdq = scr["xdq"]
            xdT = scr["xdT"]; sq_scrap = scr["sq"]
            sc_ssqv = [0] * TT
            sc_q8 = [0] * TT
            dv_xdq = [0] * TT
            dv_ev = [[0] * 4 for _ in range(TT)]
            dv_res = [[0] * 4 for _ in range(TT)]
            pe_tr = [[0] * 4 for _ in range(TT)]
            pe_mm = [[0] * 4 for _ in range(TT)]
            st_out = []
            for t in range(TT):
                p = t % 2
                w_ = list(x_ready[t])
                if t >= 2:
                    w_.append(("sc", sc_ssqv[t - 2]))
                b.op("vector", "dv", lambda e, t=t, p=p: e.tensor_reduce(
                    out=rsum[:, p:p + 1], in_=xs[:, t, :], axis=AX.X,
                    op=ALU.add, negate=True), waits=w_, risky=True)
                dv_nm = b.op("vector", "dv", lambda e, p=p: e.tensor_scalar_mul(
                    out=nmean[:, p:p + 1], in0=rsum[:, p:p + 1], scalar1=INV_D))
                sc_ssqv[t] = b.op("scalar", "sc", lambda e, t=t, p=p: e.activation(
                    out=sq_scrap[:, p, :], in_=xs[:, t, :], func=AF.Square,
                    bias=nmean[:, p:p + 1], scale=1.0, accum_out=ssq[:, p:p + 1]),
                    waits=[("dv", dv_nm)] + ([("gp", gm)] if t == 0 else []),
                    chain=True)
                sc_sd = b.op("scalar", "sc", lambda e, p=p: e.activation(
                    out=sdvv[:, p:p + 1], in_=ssq[:, p:p + 1], func=AF.Sqrt,
                    bias=epsb[:, 0:1], scale=INV_D), chain=True)
                b.op("vector", "dv", lambda e, p=p: e.reciprocal(
                    out=rstd[:, p:p + 1], in_=sdvv[:, p:p + 1]), waits=[("sc", sc_sd)],
                    risky=True)
                b.op("vector", "dv", lambda e, t=t, p=p: e.tensor_scalar(
                    out=xh[:, p, :], in0=xs[:, t, :], scalar1=nmean[:, p:p + 1],
                    scalar2=rstd[:, p:p + 1], op0=ALU.add, op1=ALU.mult), waits=w_)
                b.op("vector", "dv", lambda e, p=p: e.tensor_reduce(
                    out=amax[:, p, :],
                    in_=xh[:, p, :].rearrange("p (c k) -> p c k", k=128),
                    axis=AX.X, op=ALU.max, apply_absolute_value=True), risky=True)
                b.op("vector", "dv", lambda e, p=p: e.reciprocal(
                    out=r2[:, p, :], in_=amax[:, p, :]), risky=True)
                b.op("vector", "dv", lambda e, p=p: e.tensor_scalar_mul(
                    out=r2[:, p, :], in0=r2[:, p, :], scalar1=224.0))
                dv_r2 = b.op("vector", "dv", lambda e, p=p: e.tensor_scalar_mul(
                    out=s2[:, p, :], in0=amax[:, p, :], scalar1=1.0 / 224.0))
                for blk in range(DC):
                    sc_q8[t] = b.op("vector", "dv", lambda e, p=p, blk=blk: e.tensor_scalar(
                        out=xq8[:, p, 128 * blk:128 * (blk + 1)],
                        in0=xh[:, p, 128 * blk:128 * (blk + 1)],
                        scalar1=r2[:, p, blk:blk + 1], scalar2=None, op0=ALU.mult))
                for blk in range(DC):
                    w_ = [("pe", pe_tr[t - 2][3])] if (t >= 2 and blk == 0) else []
                    dv_xdq[t] = b.op("vector", "dv", lambda e, p=p, blk=blk: e.tensor_scalar(
                        out=xdq[:, p, 128 * blk:128 * (blk + 1)],
                        in0=xq8[:, p, 128 * blk:128 * (blk + 1)],
                        scalar1=s2[:, p, blk:blk + 1], scalar2=None, op0=ALU.mult),
                        waits=w_)
                for k in range(4):
                    bank = 6 + (k % 2)
                    for j in range(4):
                        c = 4 * k + j
                        w_ = [("dv", dv_xdq[t])] if (k == 0 and j == 0) else []
                        if j == 0:
                            w_ += b.use_bank(bank)
                        if b.paranoid:
                            w_.append(("gp", gm))
                        pe_tr[t][k] = b.op("tensor", "pe", lambda e, p=p, c=c, bank=bank, j=j: e.transpose(
                            banks[bank][:, 128 * j:128 * (j + 1)],
                            xdq[:, p, 128 * c:128 * (c + 1)], ident_b[:]), waits=w_)
                    w_ev = [("pe", pe_tr[t][k])]
                    if t >= 2 and k == 0:
                        w_ev.append(("pe", pe_mm[t - 2][3]))
                    dv_ev[t][k] = b.op("vector", "dv", lambda e, k=k, p=p, bank=bank: e.tensor_copy(
                        out=xdT[:, p, 4 * k:4 * (k + 1), :],
                        in_=banks[bank][:, 0:512].rearrange("p (j k) -> p j k", k=128)),
                        waits=w_ev)
                    b.done_bank(bank, "dv", dv_ev[t][k])
                def emit_resid(tt):
                    for oc in range(4):
                        dv_res[tt][oc] = b.op("vector", "dv",
                                              lambda e, tt=tt, oc=oc: e.scalar_tensor_tensor(
                                                  out=xs[:, tt, 512 * oc:512 * (oc + 1)],
                                                  in0=ps[oc][:, 0:512], scalar=0.5,
                                                  in1=xs[:, tt, 512 * oc:512 * (oc + 1)],
                                                  op0=ALU.mult, op1=ALU.add),
                                              waits=[("pe", pe_mm[tt][oc])])
                        b.done_bank(oc, "dv", dv_res[tt][oc])
                    if store is not None:
                        st_out.append(b.dma("gpsimd", "gs",
                                            store[128 * tt:128 * (tt + 1), :],
                                            xs[:, tt, :], waits=[("dv", dv_res[tt][3])]))
                if t >= 1:
                    emit_resid(t - 1)
                for c in range(DC):
                    for oc in range(4):
                        w_ = []
                        if c == 0:
                            w_ = [("dv", dv_ev[t][3]), w_ready] + b.use_bank(oc)
                        pe_mm[t][oc] = b.op("tensor", "pe", lambda e, p=p, c=c, oc=oc: e.matmul(
                            ps[oc][:, 0:512], xdT[:, p, c, :],
                            wT[:, c, 512 * oc:512 * (oc + 1)],
                            start=(c == 0), stop=(c == DC - 1)), waits=w_)
            emit_resid(TT - 1)
            return dv_res, st_out

        dv_res1, _ = emit_ffn(x_str, wdeqT, ("dv", W_DONE),
                              [[("lx", 16 * TT)] for t in range(TT)], scrA)
        b.flush("ph01_wq_ffn1")
        ctxA0.close()
        zT_sb = ctxA.enter_context(nc.sbuf_tensor("zT_sb", [128, DC, 1024], BF16))

        pe_ztr = [[0] * 4 for _ in range(TT)]
        dv_zev = [[0] * 4 for _ in range(TT)]
        sc_ss2 = [0] * TT
        y_sp_tiles = []
        for t in range(TT):
            p = t % 2
            w_ = [("sc", sc_ss2[t - 2])] if t >= 2 else []
            b.op("vector", "dv", lambda e, t=t, p=p: e.tensor_reduce(
                out=rsum[:, p:p + 1], in_=x_str[:, t, :], axis=AX.X,
                op=ALU.add, negate=True), waits=w_, risky=True)
            dv_nm = b.op("vector", "dv", lambda e, p=p: e.tensor_scalar_mul(
                out=nmean[:, p:p + 1], in0=rsum[:, p:p + 1], scalar1=INV_D))
            sc_ss2[t] = b.op("scalar", "sc", lambda e, t=t, p=p: e.activation(
                out=sq_scrap[:, p, :], in_=x_str[:, t, :], func=AF.Square,
                bias=nmean[:, p:p + 1], scale=1.0, accum_out=ssq[:, p:p + 1]),
                waits=[("dv", dv_nm)], chain=True)
            sc_sd = b.op("scalar", "sc", lambda e, p=p: e.activation(
                out=sdvv[:, p:p + 1], in_=ssq[:, p:p + 1], func=AF.Sqrt,
                bias=epsb[:, 0:1], scale=INV_D), chain=True)
            b.op("vector", "dv", lambda e, p=p: e.reciprocal(
                out=rstd[:, p:p + 1], in_=sdvv[:, p:p + 1]), waits=[("sc", sc_sd)],
                risky=True)
            w_ = [("pe", pe_ztr[t - 2][3])] if t >= 2 else []
            dv_z = b.op("vector", "dv", lambda e, t=t, p=p: e.tensor_scalar(
                out=xh[:, p, :], in0=x_str[:, t, :], scalar1=nmean[:, p:p + 1],
                scalar2=rstd[:, p:p + 1], op0=ALU.add, op1=ALU.mult), waits=w_)
            for k in range(4):
                bank = 4 + (k % 2)
                for j in range(4):
                    c = 4 * k + j
                    w_ = [("dv", dv_z)] if (k == 0 and j == 0) else []
                    if t == 0 and k == 0 and j == 0:
                        w_.append(("gp", gm))
                    if j == 0:
                        w_ += b.use_bank(bank)
                    pe_ztr[t][k] = b.op("tensor", "pe", lambda e, p=p, c=c, bank=bank, j=j: e.transpose(
                        ps[bank][:, 128 * j:128 * (j + 1)],
                        xh[:, p, 128 * c:128 * (c + 1)], ident_f[:]), waits=w_)
                dv_zev[t][k] = b.op("vector", "dv", lambda e, t=t, k=k, bank=bank: e.tensor_copy(
                    out=zT_sb[:, 4 * k:4 * (k + 1), 128 * t:128 * (t + 1)],
                    in_=ps[bank][:, 0:512].rearrange("p (j k) -> p j k", k=128)),
                    waits=[("pe", pe_ztr[t][k])])
                b.done_bank(bank, "dv", dv_zev[t][k])
            y_sp_tiles.append((t, dv_res1[t][3]))
        cc_ag = [0, 0]
        for half in range(2):
            gs_z = 0
            for c in range(DC):
                gs_z = b.dma("gpsimd", "gs",
                             zag_in[half][128 * c:128 * (c + 1), :],
                             zT_sb[:, c, 512 * half:512 * (half + 1)],
                             waits=[("dv", dv_zev[4 * half + 3][3])])
            cc_ag[half] = b.op("gpsimd", "cc", lambda e, half=half: e.collective_compute(
                "AllGather", ALU.bypass, replica_groups=GROUPS,
                ins=[zag_in[half].ap().opt()], outs=[zag_out[half].ap().opt()]),
                waits=[("gs", gs_z)], chain=True)
        SW_SPILL = 0
        for c in range(DC):
            SW_SPILL = b.dma("gpsimd", "ts", wdeq_spill[c, :, :], wdeqT[:, c, :],
                             waits=[("dv", W_DONE)])
        for (t, wv) in y_sp_tiles:
            b.dma("gpsimd", "vy", y_spill[128 * t:128 * (t + 1), :], x_str[:, t, :],
                  waits=[("dv", wv)])
        GS_SPILL = b.cnt["gs"]
        FENCE2 = [(s, b.cnt[s]) for s in ("ts", "vy", "gs", "pe", "dv", "sc")]
        b.flush("ph2_z_ag")

        ctxA.close()
        ctxBm = ExitStack()
        m_t = ctxBm.enter_context(nc.sbuf_tensor("m_t", [128, NH, S], BF16, side="right"))
        ctxB = ExitStack()
        q_t = ctxB.enter_context(nc.sbuf_tensor("q_t", [128, NH, S], BF16))
        k_t = ctxB.enter_context(nc.sbuf_tensor("k_t", [128, NH, S], BF16))
        v_t = ctxB.enter_context(nc.sbuf_tensor("v_t", [128, 32, 512], BF16))
        ctxB3 = ExitStack()
        zt_q = ctxB3.enter_context(nc.sbuf_tensor("zt_q", [128, DC, 512], BF16))
        wqf = ctxB3.enter_context(nc.sbuf_tensor("wqf", [128, 2, 512], F32))
        wq_b = ctxB3.enter_context(nc.sbuf_tensor("wq_b", [128, DC, 512], BF16))
        wk_b = ctxB3.enter_context(nc.sbuf_tensor("wk_b", [128, DC, 512], BF16))
        wv_b = ctxB3.enter_context(nc.sbuf_tensor("wv_b", [128, DC, 512], BF16))

        dv_w3 = 0
        nw = 0
        dv_wmat = {}
        for (wsrc, wbbuf) in ((wq_t, wq_b), (wk_t, wk_b), (wv_t, wv_b)):
            for c in range(DC):
                p = nw % 2
                ldw = b.dma("scalar", f"lw3{p}", wqf[:, p, :],
                            wsrc[128 * c:128 * (c + 1), :],
                            waits=([("dv", dv_w3)] if nw >= 2 else FENCE2))
                dv_w3 = b.op("vector", "dv",
                             lambda e, p=p, c=c, wbbuf=wbbuf: e.tensor_copy(
                                 out=wbbuf[:, c, :], in_=wqf[:, p, :]),
                             waits=[(f"lw3{p}", ldw)])
                nw += 1
            dv_wmat[id(wbbuf)] = dv_w3
        pe_zread = [0] * DC
        for hq in range(8):
            qq, half = divmod(hq, 2)
            ldz = 0
            for c in range(DC):
                w_ = [("pe", pe_zread[c])] if hq >= 1 else list(FENCE2)
                ldz = b.dma("sync", f"lz{hq % 2}", zt_q[:, c, :],
                            zag_out[half][D * qq + 128 * c:D * qq + 128 * (c + 1), :],
                            waits=w_ + [("cc", cc_ag[half])])
            zq_ready = (f"lz{hq % 2}", ldz)
            for (wb, dst) in ((wq_b, q_t), (wk_b, k_t)):
                pe_h = [0] * NH
                for c in range(DC):
                    for h in range(NH):
                        w_ = []
                        if c == 0:
                            w_ = [zq_ready, ("dv", dv_wmat[id(wb)])] + b.use_bank(h)
                        pe_h[h] = b.op("tensor", "pe",
                                       lambda e, c=c, h=h, wb=wb: e.matmul(
                                           ps[h][:, 0:512],
                                           wb[:, c, 128 * h:128 * (h + 1)],
                                           zt_q[:, c, :],
                                           start=(c == 0), stop=(c == DC - 1)),
                                       waits=w_)
                for h in range(NH):
                    dvx = b.op("vector", "dv",
                               lambda e, h=h, hq=hq, dst=dst: e.tensor_copy(
                                   out=dst[:, h, 512 * hq:512 * (hq + 1)],
                                   in_=ps[h][:, 0:512]), waits=[("pe", pe_h[h])])
                    b.done_bank(h, "dv", dvx)
            pe_g = [0] * 4
            for c in range(DC):
                for u in range(4):
                    w_ = []
                    if c == 0:
                        w_ = [zq_ready, ("dv", dv_w3)] + b.use_bank(u)
                    pe_g[u] = b.op("tensor", "pe",
                                   lambda e, c=c, u=u: e.matmul(
                                       ps[u][:, 0:512],
                                       zt_q[:, c, 128 * u:128 * (u + 1)],
                                       wv_b[:, c, 0:512],
                                       start=(c == 0), stop=(c == DC - 1)), waits=w_)
                    pe_zread[c] = pe_g[u]
            for u in range(4):
                dvx = b.op("vector", "dv", lambda e, hq=hq, u=u: e.tensor_copy(
                    out=v_t[:, 4 * hq + u, :], in_=ps[u][:, 0:512]),
                    waits=[("pe", pe_g[u])])
                b.done_bank(u, "dv", dvx)
        PH3_DV = b.cnt["dv"]
        b.flush("ph3_qkv")
        ctxB3.close()

        ctxB4 = ExitStack()
        gate_h = ctxB4.enter_context(nc.sbuf_tensor("gate_h", [128, 2, S], BF16))
        av_sb = ctxB4.enter_context(nc.sbuf_tensor("av_sb", [128, 1, S], BF16))
        p_t = ctxB4.enter_context(nc.sbuf_tensor("p_t", [128, 3, 1024], BF16))
        cs_row = ctxB4.enter_context(nc.sbuf_tensor("cs_row", [1, S], F32))
        cs128 = ctxB4.enter_context(nc.sbuf_tensor("cs128", [128, 32], F32))
        rc128 = ctxB4.enter_context(nc.sbuf_tensor("rc128", [128, 32], F32))
        recip_b = ctxB4.enter_context(nc.sbuf_tensor("recip_b", [128, S], F32))

        dv_mt = [0] * NH
        pt_pe = [0, 0, 0]
        for h in range(NH):
            ph = h % 2
            scg = 0
            for qc in range(NQC):
                bank = qc % 4
                w_ = b.use_bank(bank)
                if qc == 0:
                    w_ += [("gp", gm), ("dv", GWT_B)]
                    if h == 0:
                        w_.append(("dv", PH3_DV))
                    if h >= 2:
                        w_.append(("dv", dv_mt[h - 2]))
                peg = b.op("tensor", "pe", lambda e, h=h, qc=qc, bank=bank: e.matmul(
                    ps[bank][:, 0:512], gwt_b[:], q_t[:, h, 512 * qc:512 * (qc + 1)],
                    start=True, stop=True), waits=w_)
                scg = b.op("scalar", "sc", lambda e, ph=ph, qc=qc, bank=bank: e.activation(
                    out=gate_h[:, ph, 512 * qc:512 * (qc + 1)], in_=ps[bank][:, 0:512],
                    func=AF.Sigmoid, bias=gb_sb[:, 0:1], scale=1.0),
                    waits=[("pe", peg)] + ([("ld", 32), ("gp", gm)]
                                           if (h == 0 and qc == 0) else []),
                    chain=True)
                b.done_bank(bank, "sc", scg)
            dva = 0
            dvc = 0
            for qc in range(NQC):
                klim = 4 * qc + 4
                npair = klim // 2
                pend = []

                def flush_avcs(h=h):
                    pr, slot, pslot, sc_e = pend.pop(0)
                    ki0, ki1 = 2 * pr, 2 * pr + 1
                    klim_ = pend_klim[0]
                    pe_av = pe_cs = 0
                    for u, ki in ((0, ki0), (1, ki1)):
                        w_ = [("sc", sc_e)]
                        if ki == 0:
                            w_ += b.use_bank(4)
                        pe_av = b.op("tensor", "pe",
                                     lambda e, h=h, ki=ki, pslot=pslot, u=u, klim_=klim_: e.matmul(
                                         ps[4][:, 0:512],
                                         v_t[:, ki, 128 * h:128 * (h + 1)],
                                         p_t[:, pslot, 512 * u:512 * (u + 1)],
                                         start=(ki == 0), stop=(ki == klim_ - 1)),
                                     waits=w_)
                    for u, ki in ((0, ki0), (1, ki1)):
                        w_ = [("sc", sc_e)]
                        if ki == 0:
                            w_ += b.use_bank(5)
                        pe_cs = b.op("tensor", "pe",
                                     lambda e, ki=ki, pslot=pslot, u=u, klim_=klim_: e.matmul(
                                         ps[5][0:1, 0:512], ones_b[:],
                                         p_t[:, pslot, 512 * u:512 * (u + 1)],
                                         start=(ki == 0), stop=(ki == klim_ - 1)),
                                     waits=w_)
                    pt_pe[pslot] = pe_cs
                    return pe_av, pe_cs

                pend_klim = [klim]
                pe_av_last = pe_cs_last = 0
                for pr in range(npair):
                    ki0, ki1 = 2 * pr, 2 * pr + 1
                    off0 = max(0, 128 * ki0 - 512 * qc)
                    off1 = max(0, 128 * ki1 - 512 * qc)
                    slot = pr % 2
                    pslot = pr % 3
                    wS = b.use_bank(2 * slot) + b.use_bank(2 * slot + 1)
                    pes = b.op("tensor", "pe",
                               lambda e, h=h, qc=qc, ki0=ki0, off0=off0, slot=slot: e.matmul(
                                   P[slot][:, off0:512],
                                   k_t[:, h, 128 * ki0:128 * (ki0 + 1)],
                                   q_t[:, h, 512 * qc + off0:512 * (qc + 1)],
                                   start=True, stop=True), waits=wS)
                    pes = b.op("tensor", "pe",
                               lambda e, h=h, qc=qc, ki1=ki1, off1=off1, slot=slot: e.matmul(
                                   P[slot][:, 512 + off1:1024],
                                   k_t[:, h, 128 * ki1:128 * (ki1 + 1)],
                                   q_t[:, h, 512 * qc + off1:512 * (qc + 1)],
                                   start=True, stop=True), waits=wS)
                    dv_ms = 0
                    if ki1 >= 4 * qc:
                        wm = [("pe", pes)]
                        if h == 0 and qc == 0 and pr == 0:
                            wm.append(("gp", gm))
                        if off0 or ki0 >= 4 * qc:
                            dv_ms = b.op("vector", "dv",
                                         lambda e, slot=slot, off0=off0: e.tensor_tensor(
                                             out=P[slot][:, off0:off0 + 128],
                                             in0=P[slot][:, off0:off0 + 128],
                                             in1=mask_add[:], op=ALU.add), waits=wm)
                            wm = []
                        dv_ms = b.op("vector", "dv",
                                     lambda e, slot=slot, off1=off1: e.tensor_tensor(
                                         out=P[slot][:, 512 + off1:512 + off1 + 128],
                                         in0=P[slot][:, 512 + off1:512 + off1 + 128],
                                         in1=mask_add[:], op=ALU.add), waits=wm)
                        if off0:
                            dv_ms = b.op("vector", "dv", lambda e, slot=slot, off0=off0: e.memset(
                                P[slot][:, 0:off0], -1.0e30))
                        if off1:
                            dv_ms = b.op("vector", "dv", lambda e, slot=slot, off1=off1: e.memset(
                                P[slot][:, 512:512 + off1], -1.0e30))
                    w_e = [("pe", pes), ("pe", pt_pe[pslot]), ("dv", dv_ms)]
                    sc_e = b.op("scalar", "sc",
                                lambda e, slot=slot, pslot=pslot: e.activation(
                                    out=p_t[:, pslot, :], in_=P[slot][:, 0:1024],
                                    func=AF.Exp, bias=negC[:, 0:1], scale=SCALE),
                                chain=True, waits=w_e)
                    b.done_bank(2 * slot, "sc", sc_e)
                    b.done_bank(2 * slot + 1, "sc", sc_e)
                    pend.append((pr, slot, pslot, sc_e))
                    if len(pend) > 2:
                        pe_av_last, pe_cs_last = flush_avcs()
                while pend:
                    pe_av_last, pe_cs_last = flush_avcs()
                dva = b.op("vector", "dv", lambda e, ph=ph, qc=qc: e.tensor_copy(
                    out=av_sb[:, 0, 512 * qc:512 * (qc + 1)], in_=ps[4][:, 0:512]),
                    waits=[("pe", pe_av_last)])
                b.done_bank(4, "dv", dva)
                dvc = b.op("vector", "dv", lambda e, qc=qc: e.tensor_copy(
                    out=cs_row[0:1, 512 * qc:512 * (qc + 1)], in_=ps[5][0:1, 0:512]),
                    waits=[("pe", pe_cs_last)])
                b.done_bank(5, "dv", dvc)
            ld1 = b.dma("sync", "lcs", cs128[:, :], cs_row[0:1, :], waits=[("dv", dvc)])
            dvr = b.op("vector", "dv", lambda e: e.reciprocal(out=rc128[:, :], in_=cs128[:, :]),
                       waits=[("lcs", ld1)], risky=True)
            ld2 = b.dma("sync", "lcs", cs_row[0:1, :], rc128[:, :], waits=[("dv", dvr)])
            dv_rb = 0
            for qc in range(NQC):
                bank = qc % 4
                w_ = b.use_bank(bank) + [("lcs", ld2)]
                peb = b.op("tensor", "pe", lambda e, qc=qc, bank=bank: e.matmul(
                    ps[bank][:, 0:512], ones1f[:],
                    cs_row[0:1, 512 * qc:512 * (qc + 1)],
                    start=True, stop=True), waits=w_)
                dv_rb = b.op("vector", "dv", lambda e, qc=qc, bank=bank: e.tensor_copy(
                    out=recip_b[:, 512 * qc:512 * (qc + 1)], in_=ps[bank][:, 0:512]),
                    waits=[("pe", peb)])
                b.done_bank(bank, "dv", dv_rb)
            dvt = b.op("vector", "dv", lambda e, ph=ph: e.tensor_tensor(
                out=recip_b[:, :], in0=av_sb[:, 0, :], in1=recip_b[:, :], op=ALU.mult),
                waits=[("dv", dv_rb), ("sc", scg)])
            dv_mt[h] = b.op("vector", "dv", lambda e, h=h, ph=ph: e.tensor_tensor(
                out=m_t[:, h, :], in0=recip_b[:, :], in1=gate_h[:, ph, :], op=ALU.mult),
                waits=[("dv", dvt)])
        b.flush("ph4_attn")
        ctxB4.close()
        ctxB.close()

        ctxC = ExitStack()
        x_str2 = ctxC.enter_context(nc.sbuf_tensor("x_str2", [128, TT, D], F32))
        wdeqT2 = ctxC.enter_context(nc.sbuf_tensor("wdeqT2", [128, DC, D], BF16))
        ctxC0 = ExitStack()
        wof = ctxC0.enter_context(nc.sbuf_tensor("wof", [128, 1, D], F32))
        wo_b = ctxC0.enter_context(nc.sbuf_tensor("wo_b", [128, 4, D], BF16))
        rs_buf = ctxC0.enter_context(nc.sbuf_tensor("rs_buf", [128, 2, 2048], BF16))

        dv_wo = 0
        for c in range(4):
            ldw = b.dma("sync", "lwo", wof[:, 0, :], wo_t[128 * c:128 * (c + 1), :],
                        waits=([("dv", dv_wo)] if c >= 1 else [("dv", dv_mt[NH - 1])]))
            dv_wo = b.op("vector", "dv", lambda e, c=c: e.tensor_copy(
                out=wo_b[:, c, :], in_=wof[:, 0, :]), waits=[("lwo", ldw)])
        cc_rs = [0] * 8
        gr_last = [0, 0]
        nqt = 0
        for j in range(TT):
            for g in range(4):
                qt = 8 * g + j
                slot = nqt % 2
                dvo = 0
                for half in range(2):
                    peo = 0
                    for e2p in (2 * half, 2 * half + 1):
                        view = e2p
                        for c in range(4):
                            w_ = []
                            if c == 0:
                                w_ = b.use_bank(view)
                                if nqt == 0 and e2p == 0:
                                    w_ += [("dv", dv_wo), ("dv", dv_mt[NH - 1]),
                                           ("dv", dv_mt[NH - 2])]
                            peo = b.op("tensor", "pe",
                                       lambda e, e2p=e2p, qt=qt, c=c, view=view: e.matmul(
                                           ps[view][:, 0:512],
                                           m_t[:, c, 128 * qt:128 * (qt + 1)],
                                           wo_b[:, c, 512 * e2p:512 * (e2p + 1)],
                                           start=(c == 0), stop=(c == 3)), waits=w_)
                    Pt = P[half]
                    dvo = b.op("vector", "dv", lambda e, Pt=Pt, slot=slot, half=half: e.tensor_copy(
                        out=rs_buf[:, slot, 1024 * half:1024 * (half + 1)],
                        in_=Pt[:, 0:1024]),
                        waits=[("pe", peo), (f"gr{slot}", gr_last[slot])])
                    b.done_bank(2 * half, "dv", dvo)
                    b.done_bank(2 * half + 1, "dv", dvo)
                gr_last[slot] = b.dma("gpsimd", f"gr{slot}",
                                      rs_in[j][128 * g:128 * (g + 1), :],
                                      rs_buf[:, slot, :], waits=[("dv", dvo)])
                nqt += 1
            cc_rs[j] = b.op("gpsimd", "cc", lambda e, j=j: e.collective_compute(
                "ReduceScatter", ALU.add, replica_groups=GROUPS,
                ins=[rs_in[j].ap().opt()], outs=[rs_out[j].ap().opt()]),
                waits=[("gr0", gr_last[0]), ("gr1", gr_last[1])], chain=True)
        if taps:
            gt = b.dma("gpsimd", "gs", dbg_y[:, :], y_spill[:, :],
                       waits=[("gs", GS_SPILL)])
            gt = b.dma("gpsimd", "gs", dbg_wd[:, :, :], wdeq_spill[:, :, :])
            gt = b.dma("gpsimd", "gs", dbg_zo[:, 0:512], zag_out[0][:, :],
                       waits=[("cc", cc_ag[1])])
            gt = b.dma("gpsimd", "gs", dbg_zo[:, 512:1024], zag_out[1][:, :])
            gt = b.dma("gpsimd", "gs", dbg_rs[0:128, :], rs_out[0][:, :],
                       waits=[("cc", cc_rs[7])])
            b.wait("gpsimd", [("gs", gt)])
        b.flush("ph5_oproj_rs")
        ctxC0.close()
        ctxBm.close()
        y_buf = ctxC.enter_context(nc.sbuf_tensor("y_buf", [128, 2, D], F32))
        rs_sb = ctxC.enter_context(nc.sbuf_tensor("rs_sb", [128, 2, D], BF16))
        xh2 = ctxC.enter_context(nc.sbuf_tensor("xh2", [128, 2, D], F32))
        xq82 = ctxC.enter_context(nc.sbuf_tensor("xq82", [128, 2, D], FP8))
        xdq2 = ctxC.enter_context(nc.sbuf_tensor("xdq2", [128, 2, D], BF16))
        xdT2 = ctxC.enter_context(nc.sbuf_tensor("xdT2", [128, 2, DC, 128], BF16))
        scrC = dict(xh=xh2, xq8=xq82, xdq=xdq2, xdT=xdT2, sq=sq_scrap)

        ld_wd = 0
        for c in range(DC):
            ld_wd = b.dma("sync", "lwd", wdeqT2[:, c, :], wdeq_spill[c, :, :],
                          waits=[("ts", SW_SPILL), ("dv", dv_mt[NH - 1])])
        dv_x2 = [0] * TT
        for t in range(TT):
            p = t % 2
            w_ = [("dv", dv_x2[t - 2])] if t >= 2 else []
            ldy = b.dma("sync", f"ly{p}", y_buf[:, p, :],
                        y_spill[128 * t:128 * (t + 1), :],
                        waits=w_ + [("vy", 16 * TT), ("dv", dv_mt[NH - 1])])
            ldr = b.dma("sync", f"lr{p}", rs_sb[:, p, :],
                        rs_out[t][:, :], waits=w_ + [("cc", cc_rs[t])])
            dv_x2[t] = b.op("vector", "dv", lambda e, t=t, p=p: e.tensor_tensor(
                out=x_str2[:, t, :], in0=y_buf[:, p, :],
                in1=rs_sb[:, p, :], op=ALU.add),
                waits=[(f"ly{p}", ldy), (f"lr{p}", ldr)])
        b.flush("ph6_resid")

        _, st_out = emit_ffn(x_str2, wdeqT2, ("lwd", ld_wd),
                             [[("dv", dv_x2[t])] for t in range(TT)], scrC,
                             store=out_sh)
        b.wait("gpsimd", [("gs", st_out[-1])])
        b.flush("ph7_ffn2")
        ctxC.close()

    return nc


_cache = {}


def kernel(L0_attn_x, L0_attn_q_w, L0_attn_k_w, L0_attn_v_w, L0_attn_o_w,
           L0_attn_gate_w, L0_attn_gate_b, L0_ffn_weight):
    x = np.ascontiguousarray(np.asarray(L0_attn_x, dtype=np.float32).reshape(8192, 2048))
    qw = np.asarray(L0_attn_q_w, dtype=np.float32)
    kw = np.asarray(L0_attn_k_w, dtype=np.float32)
    vw = np.asarray(L0_attn_v_w, dtype=np.float32)
    ow = np.asarray(L0_attn_o_w, dtype=np.float32)
    gw = np.asarray(L0_attn_gate_w, dtype=np.float32)
    gbv = np.ascontiguousarray(np.asarray(L0_attn_gate_b, dtype=np.float32).reshape(128, 1))
    fw = np.ascontiguousarray(np.asarray(L0_ffn_weight, dtype=np.float32))
    owt = np.ascontiguousarray(ow.T)
    gwt_host = np.ascontiguousarray(gw.T)

    if "nc" not in _cache:
        _cache["nc"] = build()
    nc = _cache["nc"]

    in_maps = []
    for c in range(NCORES):
        bb, r = divmod(c, 4)
        sl = slice(512 * r, 512 * (r + 1))
        in_maps.append({
            "x_sh": np.ascontiguousarray(x[4096 * bb + 1024 * r: 4096 * bb + 1024 * (r + 1)]),
            "wq_t": np.ascontiguousarray(qw[sl, :].T),
            "wk_t": np.ascontiguousarray(kw[sl, :].T),
            "wv_t": np.ascontiguousarray(vw[sl, :].T),
            "wo_t": np.ascontiguousarray(owt[sl, :]),
            "gwt": gwt_host,
            "gb": gbv,
            "wf": fw,
        })
    res = run_bass_kernel_spmd(nc, in_maps, core_ids=list(range(NCORES)))
    out = np.concatenate([np.asarray(res.results[c]["out_sh"]) for c in range(NCORES)],
                         axis=0)
    return out.reshape(2, 4096, 2048).astype(np.float32)


# version 11
# speedup vs baseline: 1.4574x; 1.0280x over previous
import numpy as np
from contextlib import ExitStack

import concourse.bass as bass
import concourse.mybir as mybir
from concourse.bass_utils import run_bass_kernel_spmd

F32 = mybir.dt.float32
BF16 = mybir.dt.bfloat16
FP8 = mybir.dt.float8e4
AF = mybir.ActivationFunctionType
ALU = mybir.AluOpType
AX = mybir.AxisListType

NCORES = 8
TT = 8
D = 2048
DC = 16
S = 4096
NH = 4
QCW = 512
NQC = S // QCW
C_SHIFT = 20.0
SCALE = 0.08838834764831845
INV_D = 1.0 / 2048.0
EPS = 1e-5
GROUPS = [[0, 1, 2, 3], [4, 5, 6, 7]]


class Builder:
    def __init__(self, nc, paranoid=False):
        self.nc = nc
        self.paranoid = paranoid
        self.q = {k: [] for k in ("sync", "gpsimd", "vector", "scalar", "tensor")}
        self.sem = {}
        self.cnt = {}
        self.bank_state = [None] * 8
        self.risky = {}

    def add_sem(self, ctx, name):
        self.sem[name] = ctx.enter_context(self.nc.semaphore(name))
        self.cnt[name] = 0

    def flush(self, blockname):
        with self.nc.Block(blockname) as block:
            for eng, fns in self.q.items():
                if fns:
                    getattr(block, eng)(lambda e, fns=tuple(fns): [f(e) for f in fns])
        self.q = {k: [] for k in self.q}

    def op(self, eng, semname, fn, waits=(), inc=1, chain=False, risky=False):
        w = [(s, v) for (s, v) in waits if v > 0]
        if chain or (self.paranoid and eng == "vector"):
            prev = self.cnt[semname]
            if prev > 0:
                w.append((semname, prev))
        pend = self.risky.pop(eng, None)
        if pend is not None:
            w.append(pend)
        self.cnt[semname] += inc
        if risky:
            self.risky[eng] = (semname, self.cnt[semname])
        sems = self.sem

        def run(e, w=tuple(w), fn=fn, semname=semname, inc=inc):
            seen = {}
            for (s, v) in w:
                if seen.get(s, -1) < v:
                    e.wait_ge(sems[s], v)
                    seen[s] = v
            fn(e).then_inc(sems[semname], inc)
        self.q[eng].append(run)
        return self.cnt[semname]

    def wait(self, eng, waits):
        w = [(s, v) for (s, v) in waits if v > 0]
        sems = self.sem

        def run(e):
            for (s, v) in w:
                e.wait_ge(sems[s], v)
        self.q[eng].append(run)

    def dma(self, eng, semname, out, in_, waits=()):
        return self.op(eng, semname, lambda e: e.dma_start(out=out, in_=in_),
                       waits=waits, inc=16)

    def use_bank(self, k):
        st = self.bank_state[k]
        return [st] if st is not None else []

    def done_bank(self, k, semname, val):
        self.bank_state[k] = (semname, val)


def build(paranoid=False, taps=False):
    nc = bass.Bass("TRN2", target_bir_lowering=False)

    x_sh = nc.dram_tensor("x_sh", [1024, D], F32, kind="ExternalInput")
    wq_t = nc.dram_tensor("wq_t", [D, 512], F32, kind="ExternalInput")
    wk_t = nc.dram_tensor("wk_t", [D, 512], F32, kind="ExternalInput")
    wv_t = nc.dram_tensor("wv_t", [D, 512], F32, kind="ExternalInput")
    wo_t = nc.dram_tensor("wo_t", [512, D], F32, kind="ExternalInput")
    gwt = nc.dram_tensor("gwt", [128, 128], F32, kind="ExternalInput")
    gb = nc.dram_tensor("gb", [128, 1], F32, kind="ExternalInput")
    wf = nc.dram_tensor("wf", [D, D], F32, kind="ExternalInput")
    out_sh = nc.dram_tensor("out_sh", [1024, D], F32, kind="ExternalOutput")

    zag_in = [nc.dram_tensor(f"zag_in{i}", [D, QCW], BF16) for i in range(2)]
    zag_out = [nc.dram_tensor(f"zag_out{i}", [4 * D, QCW], BF16) for i in range(2)]
    rs_in = [nc.dram_tensor(f"rs_in{p}", [4 * 128, D], BF16) for p in range(8)]
    rs_out = [nc.dram_tensor(f"rs_out{p}", [128, D], BF16) for p in range(8)]
    y_spill = nc.dram_tensor("y_spill", [1024, D], F32)
    wdeq_spill = nc.dram_tensor("wdeq_spill", [DC, 128, D], BF16)
    if taps:
        dbg_am = nc.dram_tensor("dbg_am", [DC, 128, DC], F32, kind="ExternalOutput")
        dbg_r2 = nc.dram_tensor("dbg_r2", [DC, 128, DC], F32, kind="ExternalOutput")
        dbg_q8 = nc.dram_tensor("dbg_q8", [DC, 128, D], FP8, kind="ExternalOutput")
        dbg_dq = nc.dram_tensor("dbg_dq", [DC, 128, D], BF16, kind="ExternalOutput")
        dbg_y = nc.dram_tensor("dbg_y", [1024, D], F32, kind="ExternalOutput")
        dbg_wd = nc.dram_tensor("dbg_wd", [DC, 128, D], BF16, kind="ExternalOutput")
        dbg_zo = nc.dram_tensor("dbg_zo", [4 * D, 1024], BF16, kind="ExternalOutput")
        dbg_q = nc.dram_tensor("dbg_q", [128, NH * S], BF16, kind="ExternalOutput")
        dbg_k = nc.dram_tensor("dbg_k", [128, NH * S], BF16, kind="ExternalOutput")
        dbg_v = nc.dram_tensor("dbg_v", [128, 32 * 512], BF16, kind="ExternalOutput")
        dbg_m = nc.dram_tensor("dbg_m", [128, NH * S], BF16, kind="ExternalOutput")
        dbg_g = nc.dram_tensor("dbg_g", [128, 2 * S], BF16, kind="ExternalOutput")
        dbg_rs = nc.dram_tensor("dbg_rs", [128, D], BF16, kind="ExternalOutput")

    b = Builder(nc, paranoid=paranoid)
    with ExitStack() as ctx:
        for s in ("ld", "gs", "cc", "pe", "dv", "sc", "gp", "lx",
                  "lwf0", "lwf1", "lz0", "lz1", "lw30", "lw31", "lcs",
                  "lwo", "lwd", "ly0", "ly1", "lr0", "lr1", "gr0", "gr1",
                  "sy", "sw", "vy", "ts", "gz0", "gz1"):
            b.add_sem(ctx, s)

        P = [ctx.enter_context(nc.psum_tensor(f"P{k}", [128, 1024], F32))
             for k in range(3)]
        pt = [ctx.enter_context(nc.psum_tensor(f"pt{k}", [128, 1024], BF16))
              for k in range(2)]

        class PV:
            def __init__(self, t, off):
                self.t = t
                self.off = off

            def __getitem__(self, idx):
                rows, cols = idx
                start = self.off + (cols.start or 0)
                stop = self.off + (cols.stop if cols.stop is not None else QCW)
                return self.t[rows, start:stop]

        ps = [PV(P[k // 2], QCW * (k % 2)) for k in range(6)]
        banks = ps + pt

        ident_f = ctx.enter_context(nc.sbuf_tensor("ident_f", [128, 128], F32))
        ident_b = ctx.enter_context(nc.sbuf_tensor("ident_b", [128, 128], BF16))
        mask_add = ctx.enter_context(nc.sbuf_tensor("mask_add", [128, 128], F32))
        ones_b = ctx.enter_context(nc.sbuf_tensor("ones_b", [128, 1], BF16))
        ones1f = ctx.enter_context(nc.sbuf_tensor("ones1f", [1, 128], F32))
        negC = ctx.enter_context(nc.sbuf_tensor("negC", [128, 1], F32))
        epsb = ctx.enter_context(nc.sbuf_tensor("epsb", [128, 1], F32))
        gb_sb = ctx.enter_context(nc.sbuf_tensor("gb_sb", [128, 1], F32))
        gwt_f = ctx.enter_context(nc.sbuf_tensor("gwt_f", [128, 128], F32))
        gwt_b = ctx.enter_context(nc.sbuf_tensor("gwt_b", [128, 128], BF16))
        rsum = ctx.enter_context(nc.sbuf_tensor("rsum", [128, 2], F32))
        nmean = ctx.enter_context(nc.sbuf_tensor("nmean", [128, 2], F32))
        ssq = ctx.enter_context(nc.sbuf_tensor("ssq", [128, 2], F32))
        sdvv = ctx.enter_context(nc.sbuf_tensor("sdvv", [128, 2], F32))
        rstd = ctx.enter_context(nc.sbuf_tensor("rstd", [128, 2], F32))
        tbx = ctx.enter_context(nc.sbuf_tensor("tbx", [128, 2], F32))
        nmean2 = ctx.enter_context(nc.sbuf_tensor("nmean2", [128, 2], F32))
        ssq2 = ctx.enter_context(nc.sbuf_tensor("ssq2", [128, 2], F32))
        sdv2 = ctx.enter_context(nc.sbuf_tensor("sdv2", [128, 2], F32))
        rstd2 = ctx.enter_context(nc.sbuf_tensor("rstd2", [128, 2], F32))
        tbx2 = ctx.enter_context(nc.sbuf_tensor("tbx2", [128, 2], F32))
        amax = ctx.enter_context(nc.sbuf_tensor("amax", [128, 2, DC], F32))
        r2 = ctx.enter_context(nc.sbuf_tensor("r2", [128, 2, DC], F32))
        s2 = ctx.enter_context(nc.sbuf_tensor("s2", [128, 2, DC], F32))
        sq_scrap = ctx.enter_context(nc.sbuf_tensor("sq_scrap", [128, 1, D], BF16))

        for (tname, val) in ((ident_f, 0.0), (ident_b, 0.0), (mask_add, 0.0),
                             (ones_b, 1.0), (ones1f, 1.0), (negC, -C_SHIFT),
                             (epsb, EPS)):
            b.op("gpsimd", "gp", lambda e, t=tname, v=val: e.memset(t[:], v), chain=True)
        b.op("gpsimd", "gp", lambda e: e.affine_select(
            out=ident_f[:], in_=ident_f[:], compare_op=ALU.not_equal,
            fill=1.0, base=0, pattern=[[-1, 128]], channel_multiplier=1), chain=True)
        b.op("gpsimd", "gp", lambda e: e.affine_select(
            out=ident_b[:], in_=ident_b[:], compare_op=ALU.not_equal,
            fill=1.0, base=0, pattern=[[-1, 128]], channel_multiplier=1), chain=True)
        gm = b.op("gpsimd", "gp", lambda e: e.affine_select(
            out=mask_add[:], in_=mask_add[:], compare_op=ALU.is_ge,
            fill=-1.0e30, base=0, pattern=[[1, 128]], channel_multiplier=-1), chain=True)
        b.dma("sync", "ld", gb_sb[:], gb[:, :])
        lgw = b.dma("sync", "ld", gwt_f[:], gwt[:, :])
        GWT_B = b.op("vector", "dv", lambda e: e.tensor_copy(out=gwt_b[:], in_=gwt_f[:]),
                     waits=[("ld", lgw)])

        ctxA = ExitStack()
        x_str = ctxA.enter_context(nc.sbuf_tensor("x_str", [128, TT, D], F32))
        wdeqT = ctxA.enter_context(nc.sbuf_tensor("wdeqT", [128, DC, D], BF16))
        xh = ctxA.enter_context(nc.sbuf_tensor("xh", [128, 2, D], F32))
        xq8 = ctxA.enter_context(nc.sbuf_tensor("xq8", [128, 1, D], FP8))
        xdq = ctxA.enter_context(nc.sbuf_tensor("xdq", [128, 2, D], BF16))
        xdT = ctxA.enter_context(nc.sbuf_tensor("xdT", [128, 2, DC, 128], BF16))
        scrA = dict(xh=xh, xq8=xq8, xdq=xdq, xdT=xdT, sq=sq_scrap)
        zbuf = ctxA.enter_context(nc.sbuf_tensor("zbuf", [128, 2, 4, 128], BF16))
        zln = ctxA.enter_context(nc.sbuf_tensor("zln", [128, 2, D], BF16))
        ctxA0 = ExitStack()
        wfn = ctxA0.enter_context(nc.sbuf_tensor("wfn", [128, 2, D], F32))
        wq8 = ctxA0.enter_context(nc.sbuf_tensor("wq8", [128, 1, D], FP8))
        wdq = ctxA0.enter_context(nc.sbuf_tensor("wdq", [128, 2, D], BF16))

        ld_x = [b.dma("sync", "lx", x_str[:, t, :], x_sh[128 * t:128 * (t + 1), :])
                for t in range(TT)]

        sc_wq8 = [0] * DC
        dv_wdq = [0] * DC
        dv_wev_flat = [0] * DC
        pe_wtr = [[0] * 4 for _ in range(DC)]
        GP_INIT = gm
        for i in range(DC):
            p = i % 2
            ldwf = b.dma("sync", f"lwf{p}", wfn[:, p, :], wf[128 * i:128 * (i + 1), :],
                         waits=([("dv", sc_wq8[i - 2])] if i >= 2 else []))
            b.op("vector", "dv", lambda e, p=p: e.tensor_reduce(
                out=amax[:, p, :],
                in_=wfn[:, p, :].rearrange("p (c k) -> p c k", k=128),
                axis=AX.X, op=ALU.max, apply_absolute_value=True),
                waits=[(f"lwf{p}", ldwf)], risky=True)
            b.op("vector", "dv", lambda e, p=p: e.reciprocal(
                out=r2[:, p, :], in_=amax[:, p, :]), risky=True)
            b.op("vector", "dv", lambda e, p=p: e.tensor_scalar_mul(
                out=r2[:, p, :], in0=r2[:, p, :], scalar1=224.0), risky=True)
            dv_r2w = b.op("vector", "dv", lambda e, p=p: e.tensor_scalar_mul(
                out=s2[:, p, :], in0=amax[:, p, :], scalar1=1.0 / 224.0), risky=True)
            for blk in range(DC):
                sc_wq8[i] = b.op("vector", "dv", lambda e, p=p, blk=blk: e.tensor_scalar(
                    out=wq8[:, 0, 128 * blk:128 * (blk + 1)],
                    in0=wfn[:, p, 128 * blk:128 * (blk + 1)],
                    scalar1=r2[:, p, blk:blk + 1], scalar2=None, op0=ALU.mult))
            for blk in range(DC):
                w_ = [("pe", pe_wtr[i - 2][3])] if (i >= 2 and blk == 0) else []
                dv_wdq[i] = b.op("vector", "dv", lambda e, p=p, blk=blk: e.tensor_scalar(
                    out=wdq[:, p, 128 * blk:128 * (blk + 1)],
                    in0=wq8[:, 0, 128 * blk:128 * (blk + 1)],
                    scalar1=s2[:, p, blk:blk + 1], scalar2=None, op0=ALU.mult),
                    waits=w_)
            if taps:
                b.dma("gpsimd", "gs", dbg_am[i, :, :], amax[:, p, :],
                      waits=[("dv", dv_wdq[i])])
                b.dma("gpsimd", "gs", dbg_r2[i, :, :], r2[:, p, :])
                b.dma("gpsimd", "gs", dbg_q8[i, :, :], wq8[:, 0, :])
                b.dma("gpsimd", "gs", dbg_dq[i, :, :], wdq[:, p, :])
            for k in range(4):
                bank = 6 + (k % 2)
                for j in range(4):
                    c = 4 * k + j
                    w_ = [("dv", dv_wdq[i])] if (k == 0 and j == 0) else []
                    if j == 0:
                        w_ += b.use_bank(bank)
                    if (i == 0 and k == 0 and j == 0) or b.paranoid:
                        w_.append(("gp", GP_INIT))
                    pe_wtr[i][k] = b.op("tensor", "pe", lambda e, p=p, c=c, bank=bank, j=j: e.transpose(
                        banks[bank][:, 128 * j:128 * (j + 1)],
                        wdq[:, p, 128 * c:128 * (c + 1)], ident_b[:]),
                        waits=w_)
                dvev = b.op("vector", "dv", lambda e, i=i, k=k, bank=bank: e.tensor_copy(
                    out=wdeqT[:, 4 * k:4 * (k + 1), 128 * i:128 * (i + 1)],
                    in_=banks[bank][:, 0:512].rearrange("p (j k) -> p j k", k=128)),
                    waits=[("pe", pe_wtr[i][k])])
                b.done_bank(bank, "dv", dvev)
                dv_wev_flat[i] = dvev
        W_DONE = dvev

        def emit_ffn(xs, wT, w_ready, x_ready, scr, store=None, store_sem="gs",
                     hook=None):
            xh = scr["xh"]; xq8 = scr["xq8"]; xdq = scr["xdq"]
            xdT = scr["xdT"]; sq_scrap = scr["sq"]
            sc_sdv = [0] * TT
            sc_ssqv = [0] * TT
            sc_q8 = [0] * TT
            dv_xdq = [0] * TT
            dv_ev = [[0] * 4 for _ in range(TT)]
            dv_res = [[0] * 4 for _ in range(TT)]
            pe_tr = [[0] * 4 for _ in range(TT)]
            pe_mm = [[0] * 4 for _ in range(TT)]
            st_out = []
            for t in range(TT):
                p = t % 2
                w_ = list(x_ready[t])
                if t >= 2:
                    w_.append(("sc", sc_ssqv[t - 2]))
                b.op("vector", "dv", lambda e, t=t, p=p: e.tensor_reduce(
                    out=rsum[:, p:p + 1], in_=xs[:, t, :], axis=AX.X,
                    op=ALU.add, negate=True), waits=w_, risky=True)
                dv_nm = b.op("vector", "dv", lambda e, p=p: e.tensor_scalar_mul(
                    out=nmean[:, p:p + 1], in0=rsum[:, p:p + 1], scalar1=INV_D),
                    risky=True)
                sc_ssqv[t] = b.op("scalar", "sc", lambda e, t=t, p=p: e.activation(
                    out=sq_scrap[:, 0, :], in_=xs[:, t, :], func=AF.Square,
                    scale=1.0, accum_out=ssq[:, p:p + 1]),
                    waits=list(x_ready[t]) + ([("gp", gm)] if t == 0 else []),
                    chain=True)
                dv_mb = b.op("vector", "dv", lambda e, p=p: e.tensor_tensor(
                    out=tbx[:, p:p + 1], in0=nmean[:, p:p + 1],
                    in1=nmean[:, p:p + 1], op=ALU.mult),
                    waits=([("sc", sc_sdv[t - 2])] if t >= 2 else []), risky=True)
                dv_mb = b.op("vector", "dv", lambda e, p=p: e.tensor_scalar(
                    out=tbx[:, p:p + 1], in0=tbx[:, p:p + 1], scalar1=-1.0,
                    scalar2=EPS, op0=ALU.mult, op1=ALU.add), risky=True)
                sc_sd = b.op("scalar", "sc", lambda e, p=p: e.activation(
                    out=sdvv[:, p:p + 1], in_=ssq[:, p:p + 1], func=AF.Sqrt,
                    bias=tbx[:, p:p + 1], scale=INV_D),
                    waits=[("dv", dv_mb)], chain=True)
                sc_sdv[t] = sc_sd
                b.op("vector", "dv", lambda e, p=p: e.reciprocal(
                    out=rstd[:, p:p + 1], in_=sdvv[:, p:p + 1]), waits=[("sc", sc_sd)],
                    risky=True)
                b.op("vector", "dv", lambda e, t=t, p=p: e.tensor_scalar(
                    out=xh[:, p, :], in0=xs[:, t, :], scalar1=nmean[:, p:p + 1],
                    scalar2=rstd[:, p:p + 1], op0=ALU.add, op1=ALU.mult), waits=w_)
                b.op("vector", "dv", lambda e, p=p: e.tensor_reduce(
                    out=amax[:, p, :],
                    in_=xh[:, p, :].rearrange("p (c k) -> p c k", k=128),
                    axis=AX.X, op=ALU.max, apply_absolute_value=True), risky=True)
                b.op("vector", "dv", lambda e, p=p: e.reciprocal(
                    out=r2[:, p, :], in_=amax[:, p, :]), risky=True)
                b.op("vector", "dv", lambda e, p=p: e.tensor_scalar_mul(
                    out=r2[:, p, :], in0=r2[:, p, :], scalar1=224.0), risky=True)
                dv_r2 = b.op("vector", "dv", lambda e, p=p: e.tensor_scalar_mul(
                    out=s2[:, p, :], in0=amax[:, p, :], scalar1=1.0 / 224.0), risky=True)
                for blk in range(DC):
                    sc_q8[t] = b.op("vector", "dv", lambda e, p=p, blk=blk: e.tensor_scalar(
                        out=xq8[:, 0, 128 * blk:128 * (blk + 1)],
                        in0=xh[:, p, 128 * blk:128 * (blk + 1)],
                        scalar1=r2[:, p, blk:blk + 1], scalar2=None, op0=ALU.mult))
                for blk in range(DC):
                    w_ = [("pe", pe_tr[t - 2][3])] if (t >= 2 and blk == 0) else []
                    dv_xdq[t] = b.op("vector", "dv", lambda e, p=p, blk=blk: e.tensor_scalar(
                        out=xdq[:, p, 128 * blk:128 * (blk + 1)],
                        in0=xq8[:, 0, 128 * blk:128 * (blk + 1)],
                        scalar1=s2[:, p, blk:blk + 1], scalar2=None, op0=ALU.mult),
                        waits=w_)
                for k in range(4):
                    bank = 6 + (k % 2)
                    for j in range(4):
                        c = 4 * k + j
                        w_ = [("dv", dv_xdq[t])] if (k == 0 and j == 0) else []
                        if j == 0:
                            w_ += b.use_bank(bank)
                        if b.paranoid:
                            w_.append(("gp", gm))
                        pe_tr[t][k] = b.op("tensor", "pe", lambda e, p=p, c=c, bank=bank, j=j: e.transpose(
                            banks[bank][:, 128 * j:128 * (j + 1)],
                            xdq[:, p, 128 * c:128 * (c + 1)], ident_b[:]), waits=w_)
                    w_ev = [("pe", pe_tr[t][k])]
                    if t >= 2 and k == 0:
                        w_ev.append(("pe", pe_mm[t - 2][3]))
                    dv_ev[t][k] = b.op("vector", "dv", lambda e, k=k, p=p, bank=bank: e.tensor_copy(
                        out=xdT[:, p, 4 * k:4 * (k + 1), :],
                        in_=banks[bank][:, 0:512].rearrange("p (j k) -> p j k", k=128)),
                        waits=w_ev)
                    b.done_bank(bank, "dv", dv_ev[t][k])
                def emit_resid(tt):
                    for oc in range(4):
                        dv_res[tt][oc] = b.op("vector", "dv",
                                              lambda e, tt=tt, oc=oc: e.scalar_tensor_tensor(
                                                  out=xs[:, tt, 512 * oc:512 * (oc + 1)],
                                                  in0=ps[oc][:, 0:512], scalar=0.5,
                                                  in1=xs[:, tt, 512 * oc:512 * (oc + 1)],
                                                  op0=ALU.mult, op1=ALU.add),
                                              waits=[("pe", pe_mm[tt][oc])])
                        b.done_bank(oc, "dv", dv_res[tt][oc])
                    if store is not None:
                        st_out.append(b.dma("gpsimd", store_sem,
                                            store[128 * tt:128 * (tt + 1), :],
                                            xs[:, tt, :], waits=[("dv", dv_res[tt][3])]))
                if t >= 1:
                    emit_resid(t - 1)
                mm_first = True
                for oc in range(4):
                    for c in range(DC):
                        w_ = []
                        if c == 0:
                            w_ = [("dv", dv_ev[t][3]), w_ready[oc]] + b.use_bank(oc)
                        pe_mm[t][oc] = b.op("tensor", "pe", lambda e, p=p, c=c, oc=oc: e.matmul(
                            ps[oc][:, 0:512], xdT[:, p, c, :],
                            wT[:, c, 512 * oc:512 * (oc + 1)],
                            start=(c == 0), stop=(c == DC - 1)), waits=w_)
                if hook is not None and t >= 1:
                    hook(t - 1, dv_res[t - 1][3])
            emit_resid(TT - 1)
            if hook is not None:
                hook(TT - 1, dv_res[TT - 1][3])
            return dv_res, st_out

        W_OC = [("dv", 0)] * 4
        for oc in range(4):
            W_OC[oc] = ("dv", dv_wev_flat[4 * oc + 3])
        SW_SPILL = 0
        for c in range(DC):
            SW_SPILL = b.dma("gpsimd", "ts", wdeq_spill[c, :, :], wdeqT[:, c, :],
                             waits=[("dv", W_DONE)])
        st2 = {"sc_sd": [0] * TT, "pe_ztr": [0] * TT, "gs_zb": [[0] * 4 for _ in range(TT)],
               "cc_ag": [0, 0], "dv_z": [0] * TT}
        xh_a = scrA["xh"]

        def ln2_hook(t, dv_resv):
            p = t % 2
            b.op("vector", "dv", lambda e, t=t, p=p: e.tensor_reduce(
                out=rsum[:, p:p + 1], in_=x_str[:, t, :], axis=AX.X,
                op=ALU.add, negate=True), waits=[("dv", dv_resv)], risky=True)
            dv_nm = b.op("vector", "dv", lambda e, p=p: e.tensor_scalar_mul(
                out=nmean2[:, p:p + 1], in0=rsum[:, p:p + 1], scalar1=INV_D),
                risky=True)
            sc_ss = b.op("scalar", "sc", lambda e, t=t, p=p: e.activation(
                out=sq_scrap[:, 0, :], in_=x_str[:, t, :], func=AF.Square,
                scale=1.0, accum_out=ssq2[:, p:p + 1]),
                waits=[("dv", dv_resv)], chain=True)
            w_ = [("sc", st2["sc_sd"][t - 2])] if t >= 2 else []
            b.op("vector", "dv", lambda e, p=p: e.tensor_tensor(
                out=tbx2[:, p:p + 1], in0=nmean2[:, p:p + 1],
                in1=nmean2[:, p:p + 1], op=ALU.mult), waits=w_, risky=True)
            dv_mb = b.op("vector", "dv", lambda e, p=p: e.tensor_scalar(
                out=tbx2[:, p:p + 1], in0=tbx2[:, p:p + 1], scalar1=-1.0,
                scalar2=EPS, op0=ALU.mult, op1=ALU.add), risky=True)
            sc_sd = b.op("scalar", "sc", lambda e, p=p: e.activation(
                out=sdv2[:, p:p + 1], in_=ssq2[:, p:p + 1], func=AF.Sqrt,
                bias=tbx2[:, p:p + 1], scale=INV_D), waits=[("dv", dv_mb)], chain=True)
            st2["sc_sd"][t] = sc_sd
            b.op("vector", "dv", lambda e, p=p: e.reciprocal(
                out=rstd2[:, p:p + 1], in_=sdv2[:, p:p + 1]), waits=[("sc", sc_sd)],
                risky=True)
            dv_z = b.op("vector", "dv", lambda e, t=t, p=p: e.tensor_scalar(
                out=zln[:, p, :],
                in0=x_str[:, t, :], scalar1=nmean2[:, p:p + 1],
                scalar2=rstd2[:, p:p + 1], op0=ALU.add, op1=ALU.mult))
            st2["dv_z"][t] = dv_z
            for k in range(4):
                bank = 6 + (k % 2)
                pe_tr = 0
                for j in range(4):
                    c = 4 * k + j
                    w_ = [("dv", dv_z)] if (k == 0 and j == 0) else []
                    if j == 0:
                        w_ += b.use_bank(bank)
                    pe_tr = b.op("tensor", "pe", lambda e, p=p, c=c, bank=bank, j=j: e.transpose(
                        banks[bank][:, 128 * j:128 * (j + 1)],
                        zln[:, p, 128 * c:128 * (c + 1)], ident_b[:]), waits=w_)
                st2["pe_ztr"][t] = pe_tr
                zpar = (4 * t + k) % 2
                w_ev = [("pe", pe_tr)]
                if t >= 1 or k >= 2:
                    pk = (4 * t + k) - 2
                    w_ev.append((f"gz{zpar}", st2["gs_zb"][pk // 4][pk % 4]))
                dv_ze = b.op("vector", "dv", lambda e, zpar=zpar, bank=bank: e.tensor_copy(
                    out=zbuf[:, zpar, :, :],
                    in_=banks[bank][:, 0:512].rearrange("p (j k) -> p j k", k=128)),
                    waits=w_ev)
                b.done_bank(bank, "dv", dv_ze)
                half, tq = divmod(t, 4)
                for j in range(4):
                    c = 4 * k + j
                    st2["gs_zb"][t][k] = b.dma(
                        "gpsimd", f"gz{zpar}",
                        zag_in[half][128 * c:128 * (c + 1), 128 * tq:128 * (tq + 1)],
                        zbuf[:, zpar, j, :], waits=([("dv", dv_ze)] if j == 0 else []))
            if t == 3 or t == TT - 1:
                half = 0 if t == 3 else 1
                st2["cc_ag"][half] = b.op("gpsimd", "cc", lambda e, half=half: e.collective_compute(
                    "AllGather", ALU.bypass, replica_groups=GROUPS,
                    ins=[zag_in[half].ap().opt()], outs=[zag_out[half].ap().opt()]),
                    waits=[("gz0", b.cnt["gz0"]), ("gz1", b.cnt["gz1"])], chain=True)

        dv_res1, y_st = emit_ffn(x_str, wdeqT, W_OC,
                                 [[("lx", 16 * TT)] for t in range(TT)], scrA,
                                 store=y_spill, store_sem="vy", hook=ln2_hook)
        cc_ag = st2["cc_ag"]
        GS_SPILL = b.cnt["gs"]
        FENCE2 = [(s, b.cnt[s]) for s in ("ts", "vy", "gs", "gz0", "gz1", "pe", "dv", "sc")]
        b.flush("ph01_wq_ffn1")
        ctxA0.close()

        ctxA.close()
        ctxBm = ExitStack()
        m_t = ctxBm.enter_context(nc.sbuf_tensor("m_t", [128, NH, S], BF16, side="right"))
        ctxB = ExitStack()
        q_t = ctxB.enter_context(nc.sbuf_tensor("q_t", [128, NH, S], BF16))
        k_t = ctxB.enter_context(nc.sbuf_tensor("k_t", [128, NH, S], BF16))
        v_t = ctxB.enter_context(nc.sbuf_tensor("v_t", [128, 32, 512], BF16))
        ctxB3 = ExitStack()
        zt_q = ctxB3.enter_context(nc.sbuf_tensor("zt_q", [128, DC, 512], BF16))
        wqf = ctxB3.enter_context(nc.sbuf_tensor("wqf", [128, 2, 512], F32))
        wq_b = ctxB3.enter_context(nc.sbuf_tensor("wq_b", [128, DC, 512], BF16))
        wk_b = ctxB3.enter_context(nc.sbuf_tensor("wk_b", [128, DC, 512], BF16))
        wv_b = ctxB3.enter_context(nc.sbuf_tensor("wv_b", [128, DC, 512], BF16))

        dv_w3 = 0
        nw = 0
        dv_wmat = {}
        for (wsrc, wbbuf) in ((wq_t, wq_b), (wk_t, wk_b), (wv_t, wv_b)):
            for c in range(DC):
                p = nw % 2
                ldw = b.dma("scalar", f"lw3{p}", wqf[:, p, :],
                            wsrc[128 * c:128 * (c + 1), :],
                            waits=([("dv", dv_w3)] if nw >= 2 else FENCE2))
                dv_w3 = b.op("vector", "dv",
                             lambda e, p=p, c=c, wbbuf=wbbuf: e.tensor_copy(
                                 out=wbbuf[:, c, :], in_=wqf[:, p, :]),
                             waits=[(f"lw3{p}", ldw)])
                nw += 1
            dv_wmat[id(wbbuf)] = dv_w3
        pe_zread = [0] * DC
        for hq in range(8):
            qq, half = divmod(hq, 2)
            ldz = 0
            for c in range(DC):
                w_ = [("pe", pe_zread[c])] if hq >= 1 else list(FENCE2)
                ldz = b.dma("sync", f"lz{hq % 2}", zt_q[:, c, :],
                            zag_out[half][D * qq + 128 * c:D * qq + 128 * (c + 1), :],
                            waits=w_ + [("cc", cc_ag[half])])
            zq_ready = (f"lz{hq % 2}", ldz)
            for (wb, dst) in ((wq_b, q_t), (wk_b, k_t)):
                pe_h = [0] * NH
                for c in range(DC):
                    for h in range(NH):
                        w_ = []
                        if c == 0:
                            w_ = [zq_ready, ("dv", dv_wmat[id(wb)])] + b.use_bank(h)
                        pe_h[h] = b.op("tensor", "pe",
                                       lambda e, c=c, h=h, wb=wb: e.matmul(
                                           ps[h][:, 0:512],
                                           wb[:, c, 128 * h:128 * (h + 1)],
                                           zt_q[:, c, :],
                                           start=(c == 0), stop=(c == DC - 1)),
                                       waits=w_)
                for h in range(NH):
                    dvx = b.op("vector", "dv",
                               lambda e, h=h, hq=hq, dst=dst: e.tensor_copy(
                                   out=dst[:, h, 512 * hq:512 * (hq + 1)],
                                   in_=ps[h][:, 0:512]), waits=[("pe", pe_h[h])])
                    b.done_bank(h, "dv", dvx)
            pe_g = [0] * 4
            for c in range(DC):
                for u in range(4):
                    w_ = []
                    if c == 0:
                        w_ = [zq_ready, ("dv", dv_w3)] + b.use_bank(u)
                    pe_g[u] = b.op("tensor", "pe",
                                   lambda e, c=c, u=u: e.matmul(
                                       ps[u][:, 0:512],
                                       zt_q[:, c, 128 * u:128 * (u + 1)],
                                       wv_b[:, c, 0:512],
                                       start=(c == 0), stop=(c == DC - 1)), waits=w_)
                    pe_zread[c] = pe_g[u]
            for u in range(4):
                dvx = b.op("vector", "dv", lambda e, hq=hq, u=u: e.tensor_copy(
                    out=v_t[:, 4 * hq + u, :], in_=ps[u][:, 0:512]),
                    waits=[("pe", pe_g[u])])
                b.done_bank(u, "dv", dvx)
        PH3_DV = b.cnt["dv"]
        b.flush("ph3_qkv")
        ctxB3.close()

        ctxB4 = ExitStack()
        gate_h = ctxB4.enter_context(nc.sbuf_tensor("gate_h", [128, 2, S], BF16))
        av_sb = ctxB4.enter_context(nc.sbuf_tensor("av_sb", [128, 1, S], BF16))
        p_t = ctxB4.enter_context(nc.sbuf_tensor("p_t", [128, 3, 1024], BF16))
        cs_row = ctxB4.enter_context(nc.sbuf_tensor("cs_row", [1, S], F32))
        cs128 = ctxB4.enter_context(nc.sbuf_tensor("cs128", [128, 32], F32))
        rc128 = ctxB4.enter_context(nc.sbuf_tensor("rc128", [128, 32], F32))
        recip_b = ctxB4.enter_context(nc.sbuf_tensor("recip_b", [128, S], F32))

        dv_mt = [0] * NH
        pt_pe = [0, 0, 0]
        for h in range(NH):
            ph = h % 2
            scg = 0
            for qc in range(NQC):
                bank = qc % 4
                w_ = b.use_bank(bank)
                if qc == 0:
                    w_ += [("gp", gm), ("dv", GWT_B)]
                    if h == 0:
                        w_.append(("dv", PH3_DV))
                    if h >= 2:
                        w_.append(("dv", dv_mt[h - 2]))
                peg = b.op("tensor", "pe", lambda e, h=h, qc=qc, bank=bank: e.matmul(
                    ps[bank][:, 0:512], gwt_b[:], q_t[:, h, 512 * qc:512 * (qc + 1)],
                    start=True, stop=True), waits=w_)
                scg = b.op("scalar", "sc", lambda e, ph=ph, qc=qc, bank=bank: e.activation(
                    out=gate_h[:, ph, 512 * qc:512 * (qc + 1)], in_=ps[bank][:, 0:512],
                    func=AF.Sigmoid, bias=gb_sb[:, 0:1], scale=1.0),
                    waits=[("pe", peg)] + ([("ld", 32), ("gp", gm)]
                                           if (h == 0 and qc == 0) else []),
                    chain=True)
                b.done_bank(bank, "sc", scg)
            dva = 0
            dvc = 0
            for qc in range(NQC):
                klim = 4 * qc + 4
                npair = klim // 2
                pend = []

                def flush_avcs(h=h):
                    pr, slot, pslot, sc_e = pend.pop(0)
                    ki0, ki1 = 2 * pr, 2 * pr + 1
                    klim_ = pend_klim[0]
                    pe_av = pe_cs = 0
                    for u, ki in ((0, ki0), (1, ki1)):
                        w_ = [("sc", sc_e)]
                        if ki == 0:
                            w_ += b.use_bank(4)
                        pe_av = b.op("tensor", "pe",
                                     lambda e, h=h, ki=ki, pslot=pslot, u=u, klim_=klim_: e.matmul(
                                         ps[4][:, 0:512],
                                         v_t[:, ki, 128 * h:128 * (h + 1)],
                                         p_t[:, pslot, 512 * u:512 * (u + 1)],
                                         start=(ki == 0), stop=(ki == klim_ - 1)),
                                     waits=w_)
                    for u, ki in ((0, ki0), (1, ki1)):
                        w_ = [("sc", sc_e)]
                        if ki == 0:
                            w_ += b.use_bank(5)
                        pe_cs = b.op("tensor", "pe",
                                     lambda e, ki=ki, pslot=pslot, u=u, klim_=klim_: e.matmul(
                                         ps[5][0:1, 0:512], ones_b[:],
                                         p_t[:, pslot, 512 * u:512 * (u + 1)],
                                         start=(ki == 0), stop=(ki == klim_ - 1)),
                                     waits=w_)
                    pt_pe[pslot] = pe_cs
                    return pe_av, pe_cs

                pend_klim = [klim]
                pe_av_last = pe_cs_last = 0
                for pr in range(npair):
                    ki0, ki1 = 2 * pr, 2 * pr + 1
                    off0 = max(0, 128 * ki0 - 512 * qc)
                    off1 = max(0, 128 * ki1 - 512 * qc)
                    slot = pr % 2
                    pslot = pr % 3
                    wS = b.use_bank(2 * slot) + b.use_bank(2 * slot + 1)
                    pes = b.op("tensor", "pe",
                               lambda e, h=h, qc=qc, ki0=ki0, off0=off0, slot=slot: e.matmul(
                                   P[slot][:, off0:512],
                                   k_t[:, h, 128 * ki0:128 * (ki0 + 1)],
                                   q_t[:, h, 512 * qc + off0:512 * (qc + 1)],
                                   start=True, stop=True), waits=wS)
                    pes = b.op("tensor", "pe",
                               lambda e, h=h, qc=qc, ki1=ki1, off1=off1, slot=slot: e.matmul(
                                   P[slot][:, 512 + off1:1024],
                                   k_t[:, h, 128 * ki1:128 * (ki1 + 1)],
                                   q_t[:, h, 512 * qc + off1:512 * (qc + 1)],
                                   start=True, stop=True), waits=wS)
                    dv_ms = 0
                    if ki1 >= 4 * qc:
                        wm = [("pe", pes)]
                        if h == 0 and qc == 0 and pr == 0:
                            wm.append(("gp", gm))
                        if off0 or ki0 >= 4 * qc:
                            dv_ms = b.op("vector", "dv",
                                         lambda e, slot=slot, off0=off0: e.tensor_tensor(
                                             out=P[slot][:, off0:off0 + 128],
                                             in0=P[slot][:, off0:off0 + 128],
                                             in1=mask_add[:], op=ALU.add), waits=wm)
                            wm = []
                        dv_ms = b.op("vector", "dv",
                                     lambda e, slot=slot, off1=off1: e.tensor_tensor(
                                         out=P[slot][:, 512 + off1:512 + off1 + 128],
                                         in0=P[slot][:, 512 + off1:512 + off1 + 128],
                                         in1=mask_add[:], op=ALU.add), waits=wm)
                        if off0:
                            dv_ms = b.op("vector", "dv", lambda e, slot=slot, off0=off0: e.memset(
                                P[slot][:, 0:off0], -1.0e30))
                        if off1:
                            dv_ms = b.op("vector", "dv", lambda e, slot=slot, off1=off1: e.memset(
                                P[slot][:, 512:512 + off1], -1.0e30))
                    w_e = [("pe", pes), ("pe", pt_pe[pslot]), ("dv", dv_ms)]
                    sc_e = b.op("scalar", "sc",
                                lambda e, slot=slot, pslot=pslot: e.activation(
                                    out=p_t[:, pslot, :], in_=P[slot][:, 0:1024],
                                    func=AF.Exp, bias=negC[:, 0:1], scale=SCALE),
                                chain=True, waits=w_e)
                    b.done_bank(2 * slot, "sc", sc_e)
                    b.done_bank(2 * slot + 1, "sc", sc_e)
                    pend.append((pr, slot, pslot, sc_e))
                    if len(pend) > 2:
                        pe_av_last, pe_cs_last = flush_avcs()
                while pend:
                    pe_av_last, pe_cs_last = flush_avcs()
                dva = b.op("vector", "dv", lambda e, ph=ph, qc=qc: e.tensor_copy(
                    out=av_sb[:, 0, 512 * qc:512 * (qc + 1)], in_=ps[4][:, 0:512]),
                    waits=[("pe", pe_av_last)])
                b.done_bank(4, "dv", dva)
                dvc = b.op("vector", "dv", lambda e, qc=qc: e.tensor_copy(
                    out=cs_row[0:1, 512 * qc:512 * (qc + 1)], in_=ps[5][0:1, 0:512]),
                    waits=[("pe", pe_cs_last)])
                b.done_bank(5, "dv", dvc)
            ld1 = b.dma("sync", "lcs", cs128[:, :], cs_row[0:1, :], waits=[("dv", dvc)])
            dvr = b.op("vector", "dv", lambda e: e.reciprocal(out=rc128[:, :], in_=cs128[:, :]),
                       waits=[("lcs", ld1)], risky=True)
            ld2 = b.dma("sync", "lcs", cs_row[0:1, :], rc128[:, :], waits=[("dv", dvr)])
            dv_rb = 0
            for qc in range(NQC):
                bank = qc % 4
                w_ = b.use_bank(bank) + [("lcs", ld2)]
                peb = b.op("tensor", "pe", lambda e, qc=qc, bank=bank: e.matmul(
                    ps[bank][:, 0:512], ones1f[:],
                    cs_row[0:1, 512 * qc:512 * (qc + 1)],
                    start=True, stop=True), waits=w_)
                dv_rb = b.op("vector", "dv", lambda e, qc=qc, bank=bank: e.tensor_copy(
                    out=recip_b[:, 512 * qc:512 * (qc + 1)], in_=ps[bank][:, 0:512]),
                    waits=[("pe", peb)])
                b.done_bank(bank, "dv", dv_rb)
            dvt = b.op("vector", "dv", lambda e, ph=ph: e.tensor_tensor(
                out=recip_b[:, :], in0=av_sb[:, 0, :], in1=recip_b[:, :], op=ALU.mult),
                waits=[("dv", dv_rb), ("sc", scg)])
            dv_mt[h] = b.op("vector", "dv", lambda e, h=h, ph=ph: e.tensor_tensor(
                out=m_t[:, h, :], in0=recip_b[:, :], in1=gate_h[:, ph, :], op=ALU.mult),
                waits=[("dv", dvt)])
        b.flush("ph4_attn")
        ctxB4.close()
        ctxB.close()

        ctxC = ExitStack()
        x_str2 = ctxC.enter_context(nc.sbuf_tensor("x_str2", [128, TT, D], F32))
        wdeqT2 = ctxC.enter_context(nc.sbuf_tensor("wdeqT2", [128, DC, D], BF16))
        ctxC0 = ExitStack()
        wof = ctxC0.enter_context(nc.sbuf_tensor("wof", [128, 1, D], F32))
        wo_b = ctxC0.enter_context(nc.sbuf_tensor("wo_b", [128, 4, D], BF16))
        rs_buf = ctxC0.enter_context(nc.sbuf_tensor("rs_buf", [128, 2, 2048], BF16))

        dv_wo = 0
        for c in range(4):
            ldw = b.dma("sync", "lwo", wof[:, 0, :], wo_t[128 * c:128 * (c + 1), :],
                        waits=([("dv", dv_wo)] if c >= 1 else [("dv", dv_mt[NH - 1])]))
            dv_wo = b.op("vector", "dv", lambda e, c=c: e.tensor_copy(
                out=wo_b[:, c, :], in_=wof[:, 0, :]), waits=[("lwo", ldw)])
        cc_rs = [0] * 8
        gr_last = [0, 0]
        nqt = 0
        for j in range(TT):
            for g in range(4):
                qt = 8 * g + j
                slot = nqt % 2
                dvo = 0
                for half in range(2):
                    peo = 0
                    for e2p in (2 * half, 2 * half + 1):
                        view = e2p
                        for c in range(4):
                            w_ = []
                            if c == 0:
                                w_ = b.use_bank(view)
                                if nqt == 0 and e2p == 0:
                                    w_ += [("dv", dv_wo), ("dv", dv_mt[NH - 1]),
                                           ("dv", dv_mt[NH - 2])]
                            peo = b.op("tensor", "pe",
                                       lambda e, e2p=e2p, qt=qt, c=c, view=view: e.matmul(
                                           ps[view][:, 0:512],
                                           m_t[:, c, 128 * qt:128 * (qt + 1)],
                                           wo_b[:, c, 512 * e2p:512 * (e2p + 1)],
                                           start=(c == 0), stop=(c == 3)), waits=w_)
                    Pt = P[half]
                    dvo = b.op("vector", "dv", lambda e, Pt=Pt, slot=slot, half=half: e.tensor_copy(
                        out=rs_buf[:, slot, 1024 * half:1024 * (half + 1)],
                        in_=Pt[:, 0:1024]),
                        waits=[("pe", peo), (f"gr{slot}", gr_last[slot])])
                    b.done_bank(2 * half, "dv", dvo)
                    b.done_bank(2 * half + 1, "dv", dvo)
                gr_last[slot] = b.dma("gpsimd", f"gr{slot}",
                                      rs_in[j][128 * g:128 * (g + 1), :],
                                      rs_buf[:, slot, :], waits=[("dv", dvo)])
                nqt += 1
            cc_rs[j] = b.op("gpsimd", "cc", lambda e, j=j: e.collective_compute(
                "ReduceScatter", ALU.add, replica_groups=GROUPS,
                ins=[rs_in[j].ap().opt()], outs=[rs_out[j].ap().opt()]),
                waits=[("gr0", gr_last[0]), ("gr1", gr_last[1])], chain=True)
        if taps:
            gt = b.dma("gpsimd", "gs", dbg_y[:, :], y_spill[:, :],
                       waits=[("gs", GS_SPILL)])
            gt = b.dma("gpsimd", "gs", dbg_wd[:, :, :], wdeq_spill[:, :, :])
            gt = b.dma("gpsimd", "gs", dbg_zo[:, 0:512], zag_out[0][:, :],
                       waits=[("cc", cc_ag[1])])
            gt = b.dma("gpsimd", "gs", dbg_zo[:, 512:1024], zag_out[1][:, :])
            gt = b.dma("gpsimd", "gs", dbg_rs[:, :], rs_out[0][:, :],
                       waits=[("cc", cc_rs[7])])
            b.wait("gpsimd", [("gs", gt)])
        b.flush("ph5_oproj_rs")
        ctxC0.close()
        ctxBm.close()
        y_buf = ctxC.enter_context(nc.sbuf_tensor("y_buf", [128, 2, D], F32))
        rs_sb = ctxC.enter_context(nc.sbuf_tensor("rs_sb", [128, 2, D], BF16))
        xh2 = ctxC.enter_context(nc.sbuf_tensor("xh2", [128, 2, D], F32))
        xq82 = ctxC.enter_context(nc.sbuf_tensor("xq82", [128, 1, D], FP8))
        xdq2 = ctxC.enter_context(nc.sbuf_tensor("xdq2", [128, 2, D], BF16))
        xdT2 = ctxC.enter_context(nc.sbuf_tensor("xdT2", [128, 2, DC, 128], BF16))
        scrC = dict(xh=xh2, xq8=xq82, xdq=xdq2, xdT=xdT2, sq=sq_scrap)

        ld_wd = 0
        for c in range(DC):
            ld_wd = b.dma("sync", "lwd", wdeqT2[:, c, :], wdeq_spill[c, :, :],
                          waits=[("ts", SW_SPILL), ("dv", dv_mt[NH - 1])])
        dv_x2 = [0] * TT
        for t in range(TT):
            p = t % 2
            w_ = [("dv", dv_x2[t - 2])] if t >= 2 else []
            ldy = b.dma("sync", f"ly{p}", y_buf[:, p, :],
                        y_spill[128 * t:128 * (t + 1), :],
                        waits=w_ + [("vy", y_st[-1]), ("dv", dv_mt[NH - 1])])
            ldr = b.dma("sync", f"lr{p}", rs_sb[:, p, :],
                        rs_out[t][:, :], waits=w_ + [("cc", cc_rs[t])])
            dv_x2[t] = b.op("vector", "dv", lambda e, t=t, p=p: e.tensor_tensor(
                out=x_str2[:, t, :], in0=y_buf[:, p, :],
                in1=rs_sb[:, p, :], op=ALU.add),
                waits=[(f"ly{p}", ldy), (f"lr{p}", ldr)])
        _, st_out = emit_ffn(x_str2, wdeqT2, [("lwd", ld_wd)] * 4,
                             [[("dv", dv_x2[t])] for t in range(TT)], scrC,
                             store=out_sh)
        b.wait("gpsimd", [("gs", st_out[-1])])
        b.flush("ph7_ffn2")
        ctxC.close()

    return nc


_cache = {}


def kernel(L0_attn_x, L0_attn_q_w, L0_attn_k_w, L0_attn_v_w, L0_attn_o_w,
           L0_attn_gate_w, L0_attn_gate_b, L0_ffn_weight):
    x = np.ascontiguousarray(np.asarray(L0_attn_x, dtype=np.float32).reshape(8192, 2048))
    qw = np.asarray(L0_attn_q_w, dtype=np.float32)
    kw = np.asarray(L0_attn_k_w, dtype=np.float32)
    vw = np.asarray(L0_attn_v_w, dtype=np.float32)
    ow = np.asarray(L0_attn_o_w, dtype=np.float32)
    gw = np.asarray(L0_attn_gate_w, dtype=np.float32)
    gbv = np.ascontiguousarray(np.asarray(L0_attn_gate_b, dtype=np.float32).reshape(128, 1))
    fw = np.ascontiguousarray(np.asarray(L0_ffn_weight, dtype=np.float32))
    owt = np.ascontiguousarray(ow.T)
    gwt_host = np.ascontiguousarray(gw.T)

    if "nc" not in _cache:
        _cache["nc"] = build()
    nc = _cache["nc"]

    in_maps = []
    for c in range(NCORES):
        bb, r = divmod(c, 4)
        sl = slice(512 * r, 512 * (r + 1))
        in_maps.append({
            "x_sh": np.ascontiguousarray(x[4096 * bb + 1024 * r: 4096 * bb + 1024 * (r + 1)]),
            "wq_t": np.ascontiguousarray(qw[sl, :].T),
            "wk_t": np.ascontiguousarray(kw[sl, :].T),
            "wv_t": np.ascontiguousarray(vw[sl, :].T),
            "wo_t": np.ascontiguousarray(owt[sl, :]),
            "gwt": gwt_host,
            "gb": gbv,
            "wf": fw,
        })
    res = run_bass_kernel_spmd(nc, in_maps, core_ids=list(range(NCORES)))
    out = np.concatenate([np.asarray(res.results[c]["out_sh"]) for c in range(NCORES)],
                         axis=0)
    return out.reshape(2, 4096, 2048).astype(np.float32)
